# version 29
# baseline (speedup 1.0000x reference)
import numpy as np
from contextlib import ExitStack
from collections import deque

import concourse.bass as bass
import concourse.mybir as mybir
from concourse.bass import AP
from concourse.bass_utils import run_bass_kernel_spmd

F32 = mybir.dt.float32
BF16 = mybir.dt.bfloat16
AF = mybir.ActivationFunctionType
OP = mybir.AluOpType
AX = mybir.AxisListType

NB = 4
T = 2048
NT = 16
NW = 4
D = 1024
H = 16
PAIRS = [(0, 1), (0, 2), (0, 3), (1, 2), (1, 3), (2, 3)]
SIG = [1.0, -1.0, 1.0, 1.0, -1.0, 1.0]
SCALE = 6.0 ** (-0.5)


def make_pi():
    pi = np.zeros(1024, dtype=np.int64)
    for p in range(1024):
        q = p % 512
        pi[p] = (p // 512) * 512 + (q % 32) * 16 + (q // 32)
    return pi


def make_consts():
    c = {}
    c["idt"] = np.eye(128, dtype=np.float32)
    c["tri"] = (np.arange(128)[None, :] >= np.arange(128)[:, None]).astype(np.float32)
    perm = np.zeros((8, 64, 96), dtype=np.float32)
    for side in range(2):
        for j in range(4):
            P = perm[side * 4 + j]
            for h in range(H):
                for cc in range(6):
                    p = 5 - cc if side == 1 else cc
                    i_, j_ = PAIRS[p]
                    comp = [i_, j_, j_, i_][j]
                    P[4 * h + comp, 6 * h + cc] = 1.0
    packed = np.zeros((4, 128, 96), dtype=np.float32)
    for s in range(2):
        packed[s * 2 + 0, 0:64] = perm[s * 4 + 0]
        packed[s * 2 + 1, 0:64] = perm[s * 4 + 2]
        packed[s * 2 + 0, 64:128] = perm[s * 4 + 1]
        packed[s * 2 + 1, 64:128] = perm[s * 4 + 3]
    c["perm"] = packed
    ssq = np.zeros((96, 16), dtype=np.float32)
    for h in range(H):
        ssq[6 * h:6 * h + 6, h] = 1.0
    c["ssq"] = ssq
    sbc = np.zeros((2, 16, 96), dtype=np.float32)
    for h in range(H):
        for cc in range(6):
            sbc[0, h, 6 * h + cc] = SIG[cc] * SCALE
            sbc[1, h, 6 * h + cc] = 1.0
    c["sbc"] = sbc
    spad = np.zeros((4, 96, 128), dtype=np.float32)
    for g in range(4):
        for m in range(4):
            for cc in range(6):
                spad[g, 6 * (4 * g + m) + cc, 32 * m + cc] = 1.0
    c["spad"] = spad
    c["ones64"] = np.ones((1, 64), dtype=np.float32)
    c["cvec"] = np.tile(np.array([1e-5, 1e-24, -0.5, -1.0], dtype=np.float32), (128, 1))
    return c


class Sem:
    def __init__(self, stack, nc, name):
        self.h = stack.enter_context(nc.semaphore(name))
        self.n = 0
    def inc(self, instr, k=1):
        instr.then_inc(self.h, k)
        self.n += k
        return self.n
    def wait(self, eng, n=None):
        eng.wait_ge(self.h, self.n if n is None else n)


def mkap(sl, extra_offset, free_dims, nparts=None):
    p = list(sl.ap[0])
    if nparts is not None:
        p[1] = nparts
    return AP(sl.tensor, sl.offset + extra_offset, [p] + [list(d) for d in free_dims])


def pi_out_ap(tens):
    return mkap(tens, 0, [[512, 2], [1, 32], [32, 16]])


def build(nb=NB, debug=False):
    nc = bass.Bass("TRN2", target_bir_lowering=False)
    def din(name, shape, dtype=F32):
        return nc.dram_tensor(name, list(shape), dtype, kind="ExternalInput").ap()

    x_d = din("x", [nb, T, D])
    wqkT_d = din("wqkT", [D, 256])
    wvT_d = din("wvT", [D, D])
    woutT_d = din("woutT", [D, D])
    bv_d = din("bv", [D])
    bout_d = din("bout", [D])
    bri_d = din("bri", [D])
    g1a_d = din("g1a", [2, 4, 1024])
    g2b_d = din("g2b", [2, 4, 256])
    idt_d = din("c_idt", [128, 128])
    tri_d = din("c_tri", [128, 128])
    perm_d = din("c_perm", [4, 128, 96])
    ssq_d = din("c_ssq", [96, 16])
    sbc_d = din("c_sbc", [2, 16, 96])
    spad_d = din("c_spad", [4, 96, 128])
    ones64_d = din("c_ones64", [1, 64])
    cvec_d = din("c_cvec", [128, 4])

    y_d = nc.dram_tensor("y", [nb, T, D], F32, kind="ExternalOutput").ap()
    tascr_d = nc.dram_tensor("tascr", [2, 1024, 256], F32, kind="Internal").ap()
    rscr_d = nc.dram_tensor("rscr", [2, 1024], F32, kind="Internal").ap()

    st = ExitStack()
    sb = lambda name, shape, dtype: st.enter_context(nc.sbuf_tensor(name, shape, dtype))
    with st:
        xnT = sb("xnT", [128, 8 * 2048], BF16)
        vaug = sb("vaug", [128, 16 * 1040], BF16)
        oT = sb("oT", [128, 8 * 2048], BF16)
        wqk_sb = sb("wqk", [128, 8 * 256], BF16)
        wv_sb = sb("wv", [128, 8 * 1024], BF16)
        wout_sb = sb("wout", [128, 8 * 1024], BF16)
        wr_sb = sb("wr", [128, 4 * 512], BF16)
        wi_sb = sb("wi", [128, 4 * 512], BF16)
        negscr = sb("negscr", [128, 512], BF16)
        bv_bc = sb("bvbc", [128, 1024], BF16)
        ball_bc = sb("ballbc", [128, 1024], F32)
        idt_sb = sb("idt", [128, 128], F32)
        tri_sb = sb("tri", [128, 128], BF16)
        perm_sb = sb("perm", [128, 4 * 96], F32)
        ssq_sb = sb("ssq", [96, 16], F32)
        sbc_sb = sb("sbc", [16, 2 * 96], F32)
        spad_sb = sb("spad", [96, 4 * 128], BF16)

        cvec = sb("cvec", [128, 4], F32)
        QK_sb = sb("qk", [128, 1024], F32)
        SCR = sb("scr", [128, 1536], F32)
        LqN = sb("lqn", [128, 2048], BF16)
        LkN = sb("lkn", [128, 2048], BF16)
        fq_pad = sb("fqp", [128, 2048], BF16)
        fk_pad = sb("fkp", [128, 2048], BF16)
        exp_sb = sb("exps", [128, 2048], BF16)
        xstage = sb("xst", [128, 1024], F32)
        xnstage = sb("xnst", [128, 1024], F32)
        x2stage = sb("x2st", [128, 1024], F32)
        ostage = sb("ost", [128, 2 * 1024], F32)
        cdiv = sb("cdiv", [64, 1024], F32)
        bscr = sb("bscr", [16, 512], F32)
        stat = sb("stat", [128, 8], F32)

        S_ps = st.enter_context(nc.psum_tensor("S", [128, 2048], F32))
        O4 = st.enter_context(nc.psum_tensor("O4", [65, 1024], F32))
        F6 = st.enter_context(nc.psum_tensor("F6", [128, 512], F32))
        F7 = st.enter_context(nc.psum_tensor("F7", [128, 512], F32))
        FB = [F6, F7]

        sems = {}
        for name in ["dwp", "dxp", "dws", "dyt", "dx", "dx2", "dyp", "dya", "dbo", "dbi", "dtr", "t", "v", "s"]:
            sems[name] = Sem(st, nc, "sem_" + name)
        dwp, dxp, dws, dyt = sems["dwp"], sems["dxp"], sems["dws"], sems["dyt"]
        dx, dx2, dyp, dya = sems["dx"], sems["dx2"], sems["dyp"], sems["dya"]
        dbo, dbi = sems["dbo"], sems["dbi"]
        dtr = sems["dtr"]
        tS, vS, sS = sems["t"], sems["v"], sems["s"]
        TE, VE, SE, GE, SY = nc.tensor, nc.vector, nc.scalar, nc.gpsimd, nc.sync

        stx = {
            "xst_r": 0,
            "x2_r": 0,
            "xn_r": 0,
            "f_w": [0, 0],
            "f_r": [0, 0],
            "f_rs": [0, 0],
            "qk_w": 0,
            "xnT_tile": [0] * NT,
            "lq_done": 0,
            "vaug_grp": [0] * 4,
            "sfree_s": [0, 0],
            "sfree_v": [0, 0],
            "o4_free": 0,
            "ot_r": 0,
            "odiv": 0,
            "exp_last": 0,
            "av_prev": [0, 0],
            "ost_rp": [0, 0],
            "ost_ra": [0, 0],
            "dyp_last": 0, "dya_last": 0,
            "unfold_t": 0,
        }

        gloads = [(tri_sb[:], tri_d),
                  (bv_bc[:], bv_d.rearrange("(one d) -> one d", one=1).broadcast_to([128, 1024]))]
        for kc in range(8):
            gloads.append((wqk_sb[:, kc * 256:(kc + 1) * 256], wqkT_d[kc * 128:(kc + 1) * 128, :]))
            gloads.append((wv_sb[:, kc * 1024:(kc + 1) * 1024], wvT_d[kc * 128:(kc + 1) * 128, :]))
            gloads.append((wout_sb[:, kc * 1024:(kc + 1) * 1024], woutT_d[kc * 128:(kc + 1) * 128, :]))
        for g in range(4):
            gloads.append((spad_sb[:, g * 128:(g + 1) * 128], spad_d[g]))
        for out_ap, in_ap in gloads:
            dwp.inc(GE.dma_start(out_ap, in_ap), 16)
        sloads = [
            (idt_sb[:], idt_d),
            (ssq_sb[:], ssq_d),
            (cvec[:], cvec_d),
            (ostage[:, 0:1024], bout_d.rearrange("(one d) -> one d", one=1).broadcast_to([128, 1024])),
            (ostage[:, 1024:2048], bri_d.rearrange("(one d) -> one d", one=1).broadcast_to([128, 1024])),
            (SCR[0:4, 0:1024], g1a_d[0]),
            (cdiv[0:4, 0:1024], g1a_d[1]),
            (QK_sb[0:4, 512:768], g2b_d[0]),
            (QK_sb[0:4, 768:1024], g2b_d[1]),
        ]
        for s8 in range(4):
            sloads.append((perm_sb[:, s8 * 96:(s8 + 1) * 96], perm_d[s8]))
        for s2 in range(2):
            sloads.append((sbc_sb[:, s2 * 96:(s2 + 1) * 96], sbc_d[s2]))
        for out_ap, in_ap in sloads:
            dxp.inc(SY.dma_start(out_ap, in_ap), 16)
        dxp.wait(VE)
        vS.inc(VE.tensor_tensor(ball_bc[:], ostage[:, 0:1024], ostage[:, 1024:2048], OP.add))
        dxp.wait(TE)
        dxp.wait(SE)

        def unf_step(ri, mt):
            half = mt % 2
            Fb = FB[half]
            qp = QK_sb[:, half * 256:(half + 1) * 256]
            dsem = dyt if half == 0 else dtr
            vS.wait(TE, stx["f_r"][half])
            sS.wait(TE, stx["f_rs"][half])
            src_g = SCR if ri == 0 else cdiv
            tS.inc(TE.matmul(Fb[:, 0:256],
                             src_g[0:4, mt * 128:(mt + 1) * 128],
                             QK_sb[0:4, 512 + ri * 256: 768 + ri * 256],
                             start=True, stop=True))
            stx["f_w"][half] = tS.n
            stx["unfold_t"] = tS.n
            tS.wait(VE, stx["f_w"][half])
            dsem.wait(VE)
            stx["f_r"][half] = vS.inc(VE.tensor_copy(qp, mkap(Fb[:, 0:256], 0, [[1, 16], [16, 16]])))
            vS.wait(GE, stx["f_r"][half])
            dsem.inc(GE.dma_start(tascr_d[ri, mt * 128:(mt + 1) * 128, :], qp), 16)

        def unf_fin():
            dyt.wait(GE)
            dtr.wait(GE)
            for ri, wdst in [(0, wr_sb), (1, wi_sb)]:
                base = tascr_d[ri]
                for g in range(16):
                    in_ap = AP(base.tensor, base.offset + g * 16, [[256, 32], [8192, 32], [1, 16]])
                    out_sl = wdst[(g % 4) * 32:(g % 4) * 32 + 32,
                                  (g // 4) * 512:(g // 4) * 512 + 512]
                    dws.inc(GE.dma_start(mkap(out_sl, 0, [[16, 32], [1, 16]]), in_ap), 16)

        unf_steps = deque([(lambda ri=ri, mt=mt: unf_step(ri, mt))
                           for ri in range(2) for mt in range(8)] + [unf_fin])

        def A1_tile(b1, tt):
            vS.wait(SY, stx["xst_r"])
            dx.inc(SY.dma_start(xstage[:], x_d[b1, tt * 128:(tt + 1) * 128, :]), 16)
            dx.wait(VE)
            dx2.wait(VE)
            vS.inc(VE.tensor_tensor(x2stage[:], xstage[:], xstage[:], OP.mult))
            vS.inc(VE.reduce_sum(stat[:, 0:1], xstage[:], axis=AX.X))
            vS.wait(VE)
            vS.inc(VE.reduce_sum(stat[:, 1:2], x2stage[:], axis=AX.X))
            stx["x2_r"] = vS.n
            vS.inc(VE.tensor_scalar(stat[:, 2:3], stat[:, 0:1], 1.0 / 1024, None, OP.mult))
            vS.wait(VE)
            vS.inc(VE.tensor_scalar(stat[:, 3:4], stat[:, 1:2], 1.0 / 1024, None, OP.mult))
            vS.inc(VE.tensor_tensor(stat[:, 4:5], stat[:, 2:3], stat[:, 2:3], OP.mult))
            vS.wait(VE)
            var_s = vS.inc(VE.tensor_tensor(stat[:, 5:6], stat[:, 3:4], stat[:, 4:5], OP.subtract))
            vS.wait(SE, var_s)
            sS.inc(SE.activation(stat[:, 6:7], stat[:, 5:6], AF.Ln, bias=cvec[:, 0:1]))
            sS.wait(SE)
            rstd_s = sS.inc(SE.activation(stat[:, 7:8], stat[:, 6:7], AF.Exp, scale=cvec[:, 2:3]))
            sS.wait(VE, rstd_s)
            vS.wait(VE)
            tS.wait(VE, stx.get("xn_r", 0))
            dx2.wait(VE)
            xn_s = vS.inc(VE.tensor_scalar(pi_out_ap(xnstage[:]), xstage[:], stat[:, 2:3],
                                           stat[:, 7:8], OP.subtract, OP.mult))
            stx["xst_r"] = xn_s
            stx["a_xn"] = xn_s

        def A2_tile(b1, tt):
            xn_s = stx["a_xn"]
            for half in range(2):
                Fb = FB[half]
                vS.wait(TE, max(xn_s, stx["f_r"][half]))
                sS.wait(TE, stx["f_rs"][half])
                for c4 in range(4):
                    c = half * 4 + c4
                    tS.inc(TE.transpose(Fb[:, c4 * 128:(c4 + 1) * 128],
                                        xnstage[:, c * 128:(c + 1) * 128], idt_sb[:]))
                stx["f_w"][half] = tS.n
                tS.wait(VE, stx["f_w"][half])
                out_ap = mkap(xnT[:], (half * 4) * 2048 + tt * 128, [[2048, 4], [1, 128]])
                stx["f_r"][half] = vS.inc(VE.tensor_copy(out_ap, Fb[:, 0:512]))
            stx["xn_r"] = tS.n
            stx["xnT_tile"][tt] = stx["f_r"][1]

        def B1_chunk(b1, ch):
            if not stx.get("gl_ok"):
                dwp.wait(TE)
                dwp.wait(VE)
                dyt.wait(VE)
                dtr.wait(VE)
                stx["gl_ok"] = True
            vS.wait(TE, max(stx["xnT_tile"][4 * ch + 3], stx["f_r"][0], stx["f_r"][1]))
            sS.wait(TE, max(stx["f_rs"][0], stx["f_rs"][1]))
            for mt in range(2):
                for kc in range(8):
                    ins = TE.matmul(FB[mt][:, 0:512],
                                    wqk_sb[:, kc * 256 + mt * 128: kc * 256 + mt * 128 + 128],
                                    xnT[:, kc * 2048 + ch * 512: kc * 2048 + ch * 512 + 512],
                                    start=(kc == 0), stop=(kc == 7))
                tS.inc(ins)
                stx["f_w"][mt] = tS.n
            for mt in range(2):
                tS.wait(VE, stx["f_w"][mt])
                stx["f_r"][mt] = vS.inc(VE.tensor_copy(QK_sb[:, mt * 512:(mt + 1) * 512],
                                                       FB[mt][:, 0:512]))
            stx["qk_w"] = vS.n

        def B2_chunk(b1, ch, s):
            c0, c1, c2 = (SCR[0:96, i * 512:(i + 1) * 512] for i in range(3))
            c3 = c2
            if True:
                a1 = QK_sb[0:64, s * 512:s * 512 + 512]
                a2 = QK_sb[64:128, s * 512:s * 512 + 512]
                vS.wait(TE, max(stx["qk_w"], stx["f_r"][0], stx["f_r"][1]))
                sS.wait(TE, max(stx["f_rs"][0], stx["f_rs"][1]))
                pU = perm_sb[0:64, (s * 2 + 0) * 96:(s * 2 + 0) * 96 + 96]
                pV = perm_sb[64:128, (s * 2 + 0) * 96:(s * 2 + 0) * 96 + 96]
                w6 = tS.inc(TE.matmul(F6[0:96, 0:512], pU, a1, start=True, stop=True))
                w7 = tS.inc(TE.matmul(F7[0:96, 0:512], pV, a2, start=True, stop=True))
                tS.wait(VE, w6)
                vS.inc(VE.tensor_copy(c0, F6[0:96, 0:512]))
                stx["f_r"][0] = vS.n
                tS.wait(VE, w7)
                vS.wait(VE)
                vS.inc(VE.tensor_tensor(c1, c0, F7[0:96, 0:512], OP.mult))
                stx["f_r"][1] = vS.n
                vS.wait(TE, max(stx["f_r"][0], stx["f_r"][1]))
                pU2 = perm_sb[0:64, (s * 2 + 1) * 96:(s * 2 + 1) * 96 + 96]
                pV2 = perm_sb[64:128, (s * 2 + 1) * 96:(s * 2 + 1) * 96 + 96]
                w6 = tS.inc(TE.matmul(F6[0:96, 0:512], pU2, a1, start=True, stop=True))
                w7 = tS.inc(TE.matmul(F7[0:96, 0:512], pV2, a2, start=True, stop=True))
                tS.wait(VE, w6)
                vS.inc(VE.tensor_copy(c2, F6[0:96, 0:512]))
                stx["f_r"][0] = vS.n
                tS.wait(VE, w7)
                vS.wait(VE)
                vS.inc(VE.tensor_tensor(c0, c2, F7[0:96, 0:512], OP.mult))
                stx["f_r"][1] = vS.n
                vS.wait(VE)
                vS.inc(VE.tensor_tensor(c3, c1, c0, OP.subtract))
                vS.wait(VE)
                l2_s = vS.inc(VE.tensor_tensor(c0, c3, c3, OP.mult))
                vS.wait(TE, max(l2_s, stx["f_r"][0]))
                n2_s = tS.inc(TE.matmul(F6[0:16, 0:512], ssq_sb[:], c0, start=True, stop=True))
                tS.wait(SE, n2_s)
                sS.inc(SE.activation(SCR[96:112, 0:512], F6[0:16, 0:512], AF.Ln, bias=cvec[96:112, 1:2]))
                stx["f_rs"][0] = sS.n
                sS.wait(SE)
                rn_s = sS.inc(SE.activation(bscr[:], SCR[96:112, 0:512], AF.Exp,
                                            scale=cvec[96:112, 2:3]))
                sS.wait(TE, rn_s)
                vS.wait(TE, stx["f_r"][1])
                bcn_s = tS.inc(TE.matmul(F7[0:96, 0:512], sbc_sb[:, s * 96:s * 96 + 96],
                                         bscr[:], start=True, stop=True))
                tS.wait(VE, bcn_s)
                vS.wait(VE)
                dst = (LqN if s == 0 else LkN)
                stx["f_r"][1] = vS.inc(VE.tensor_tensor(dst[0:96, ch * 512:ch * 512 + 512],
                                                        c3, F7[0:96, 0:512], OP.mult))
            if ch == 3 and s == 1:
                stx["lq_done"] = vS.n

        def V_group(b1, g, cav_snap):
            for tt in range(NT):
                p = tt % 2
                Fb = FB[p]
                vS.wait(TE, max(stx["f_r"][p], stx["xnT_tile"][tt]))
                sS.wait(TE, stx["f_rs"][p])
                for kc in range(8):
                    ins = TE.matmul(Fb[:, 0:256],
                                    xnT[:, kc * 2048 + tt * 128: kc * 2048 + tt * 128 + 128],
                                    wv_sb[:, kc * 1024 + g * 256: kc * 1024 + g * 256 + 256],
                                    start=(kc == 0), stop=(kc == 7))
                tS.inc(ins)
                stx["f_w"][p] = tS.n
                tS.wait(VE, max(stx["f_w"][p], cav_snap))
                base = tt * 1040 + (4 * g) * 65
                vaug_ap = mkap(vaug[:], base, [[65, 4], [1, 64]])
                bvs = mkap(bv_bc[:], g * 256, [[64, 4], [1, 64]])
                vS.inc(VE.tensor_tensor(vaug_ap, Fb[:, 0:256], bvs, OP.add))
                ones_ap = mkap(vaug[:], base + 64, [[65, 4]])
                stx["f_r"][p] = vS.inc(VE.memset(ones_ap, 1.0))
            stx["vaug_grp"][g] = vS.n

        def FFN_pass1(b1, tt):
            if not stx.get("w_ok"):
                dws.wait(TE)
                stx["w_ok"] = True
            vS.wait(SY, stx["x2_r"])
            dx2.inc(SY.dma_start(x2stage[:], x_d[b1, tt * 128:(tt + 1) * 128, :]), 16)
            dx2s = dx2.n
            vS.wait(TE, max(stx["f_r"][0], stx["f_r"][1], stx["xnT_tile"][tt]))
            sS.wait(TE, max(stx["f_rs"][0], stx["f_rs"][1]))
            vS.wait(VE, stx["xnT_tile"][tt])
            tS.wait(VE, stx.get("neg_r", 0))
            for kc in range(4):
                vS.inc(VE.tensor_scalar(negscr[:, kc * 128:(kc + 1) * 128],
                                        xnT[:, (kc + 4) * 2048 + tt * 128: (kc + 4) * 2048 + tt * 128 + 128],
                                        -1.0, None, OP.mult))
            neg_s = vS.n
            vS.wait(TE, neg_s)
            for kc in range(8):
                rhs = (wr_sb if kc < 4 else wi_sb)[:, (kc % 4) * 512:(kc % 4) * 512 + 512]
                lhsT = (xnT[:, kc * 2048 + tt * 128: kc * 2048 + tt * 128 + 128] if kc < 4
                        else negscr[:, (kc - 4) * 128:(kc - 3) * 128])
                ins = TE.matmul(F6[:, 0:512], lhsT,
                                rhs, start=(kc == 0), stop=(kc == 7))
            tS.inc(ins)
            stx["f_w"][0] = tS.n
            stx["neg_r"] = tS.n
            for kc in range(8):
                rhs = (wi_sb if kc < 4 else wr_sb)[:, (kc % 4) * 512:(kc % 4) * 512 + 512]
                ins = TE.matmul(F7[:, 0:512],
                                xnT[:, kc * 2048 + tt * 128: kc * 2048 + tt * 128 + 128],
                                rhs, start=(kc == 0), stop=(kc == 7))
            tS.inc(ins)
            stx["f_w"][1] = tS.n
            tS.wait(VE, stx["f_w"][0])
            dyp.wait(VE, stx["ost_rp"][0])
            dya.wait(VE, stx["ost_ra"][0])
            vS.inc(VE.tensor_tensor(ostage[:, 0:512], F6[:, 0:512], ball_bc[:, 0:512], OP.add))
            stx["f_r"][0] = vS.n
            tS.wait(VE, stx["f_w"][1])
            vS.inc(VE.tensor_tensor(ostage[:, 512:1024], F7[:, 0:512], ball_bc[:, 512:1024], OP.add))
            stx["f_r"][1] = vS.n
            dx2.wait(VE, dx2s)
            dyp.wait(VE, stx["ost_rp"][1])
            dya.wait(VE, stx["ost_ra"][1])
            vS.wait(VE)
            o2_s = vS.inc(VE.tensor_tensor(ostage[:, 1024:2048], ostage[:, 0:1024], x2stage[:], OP.add))
            stx["x2_r"] = o2_s
            vS.wait(GE, o2_s)
            dyp.wait(GE, stx["dyp_last"])
            stx["dyp_last"] = dyp.inc(GE.dma_start(y_d[b1, tt * 128:(tt + 1) * 128, :],
                                                   ostage[:, 1024:2048]), 16)
            stx["ost_rp"][1] = stx["dyp_last"]

        def OutProjPart(b, g, tt):
            p = tt % 2
            vS.wait(TE, max(stx["f_r"][0], stx["f_r"][1], stx["odiv"]))
            sS.wait(TE, max(stx["f_rs"][0], stx["f_rs"][1]))
            for half in range(2):
                Fb = FB[half]
                for k2 in range(2):
                    pt = 2 * g + k2
                    ins = TE.matmul(Fb[:, 0:512],
                                    oT[:, pt * 2048 + tt * 128: pt * 2048 + tt * 128 + 128],
                                    wout_sb[:, pt * 1024 + half * 512: pt * 1024 + half * 512 + 512],
                                    start=(k2 == 0), stop=(k2 == 1))
                tS.inc(ins)
                stx["f_w"][half] = tS.n
            for half in range(2):
                tS.wait(VE, stx["f_w"][half])
                if half == 0:
                    dyp.wait(VE, stx["ost_rp"][p])
                    dya.wait(VE, stx["ost_ra"][p])
                stx["f_r"][half] = vS.inc(VE.tensor_copy(
                    ostage[:, p * 1024 + half * 512: p * 1024 + half * 512 + 512], FB[half][:, 0:512]))
            cp_s = vS.n
            vS.wait(GE, cp_s)
            dya.wait(GE, stx["dya_last"])
            stx["dya_last"] = dya.inc(GE.dma_start(y_d[b, tt * 128:(tt + 1) * 128, :],
                                                   ostage[:, p * 1024:(p + 1) * 1024],
                                                   accum_op=OP.add), 16)
            stx["ost_ra"][p] = stx["dya_last"]

        def burst_outproj(b):
            for tt in range(NT):
                p = tt % 2
                Pp = S_ps[:, p * 1024:(p + 1) * 1024]
                sS.wait(TE, stx["exp_last"])
                vS.wait(TE, max(stx["odiv"], stx["sfree_v"][p]))
                for half in range(2):
                    for pt in range(8):
                        ins = TE.matmul(Pp[:, half * 512:half * 512 + 512],
                                        oT[:, pt * 2048 + tt * 128: pt * 2048 + tt * 128 + 128],
                                        wout_sb[:, pt * 1024 + half * 512: pt * 1024 + half * 512 + 512],
                                        start=(pt == 0), stop=(pt == 7))
                tS.inc(ins)
                pw = tS.n
                tS.wait(VE, pw)
                dyp.wait(VE, stx["ost_rp"][p])
                dya.wait(VE, stx["ost_ra"][p])
                cp_s = vS.inc(VE.tensor_copy(ostage[:, p * 1024:(p + 1) * 1024], Pp))
                stx["sfree_v"][p] = cp_s
                vS.wait(GE, cp_s)
                dya.wait(GE, stx["dya_last"])
                stx["dya_last"] = dya.inc(GE.dma_start(y_d[b, tt * 128:(tt + 1) * 128, :],
                                                       ostage[:, p * 1024:(p + 1) * 1024],
                                                       accum_op=OP.add), 16)
                stx["ost_ra"][p] = stx["dya_last"]
            stx["ot_r"] = tS.n

        def C_scatter(g):
            for s, (srcF, dstp) in enumerate([(LqN, fq_pad), (LkN, fk_pad)]):
                for ch in range(4):
                    pset = (ch + 4 * s) % 2
                    sS.wait(TE, stx["sfree_s"][pset])
                    vS.wait(TE, max(stx["lq_done"], stx["sfree_v"][pset]))
                    w_s = tS.inc(TE.matmul(S_ps[:, pset * 1024:pset * 1024 + 512],
                                           spad_sb[:, g * 128:(g + 1) * 128],
                                           srcF[0:96, ch * 512:ch * 512 + 512],
                                           start=True, stop=True))
                    tS.wait(VE, w_s)
                    stx["sfree_v"][pset] = vS.inc(
                        VE.tensor_copy(dstp[:, ch * 512:ch * 512 + 512],
                                       S_ps[:, pset * 1024:pset * 1024 + 512]))

        def C_pair(b, g, pr, w, pace=None):
            last_kt = 4 * w + 3
            nkt = 4 * w + 4
            exp_snap = {}
            av_snap = {-2: stx["av_prev"][0], -1: stx["av_prev"][1]}
            sc_snap = {}

            def emit_scores(kt):
                ss = kt % 2
                soff = max(kt * 128 - w * 512, 0)
                if kt >= 2:
                    sS.wait(TE, exp_snap[kt - 2])
                else:
                    sS.wait(TE, stx["sfree_s"][ss])
                    vS.wait(TE, max(stx["sfree_v"][ss], stx["vaug_grp"][g]))
                for j in range(2):
                    m4 = 2 * pr + j
                    tS.inc(TE.matmul(
                        S_ps[:, ss * 1024 + j * 512 + soff: ss * 1024 + j * 512 + 512],
                        fk_pad[32 * m4:32 * m4 + 6, kt * 128:kt * 128 + 128],
                        fq_pad[32 * m4:32 * m4 + 6, w * 512 + soff:w * 512 + 512],
                        start=True, stop=True, tile_position=(32 * m4, 0)))
                sc_snap[kt] = tS.n

            emit_scores(0)
            if nkt > 1:
                emit_scores(1)
            if stx.get("pending_div") is not None:
                stx["pending_div"]()
                stx["pending_div"] = None
            avj = {}
            for kt in range(nkt):
                off = max(kt * 128 - w * 512, 0)
                ss = kt % 2
                for j in range(2):
                    base = ss * 1024 + j * 512 + off
                    end = ss * 1024 + (j + 1) * 512
                    tS.wait(SE, max(avj.get((kt - 2, j), av_snap.get(kt - 2, 0)), sc_snap[kt]))
                    sS.inc(SE.activation(exp_sb[:, base:end], S_ps[:, base:end], AF.Exp))
                    ej = sS.n
                    if kt >= 4 * w:
                        sS.wait(VE, ej)
                        in0 = exp_sb[:, base: base + 128]
                        mk = vS.inc(VE.tensor_tensor(in0, in0, tri_sb[:, 0:128], OP.mult))
                        vS.wait(TE, mk)
                    else:
                        sS.wait(TE, ej)
                    if kt == 0 and j == 0:
                        vS.wait(TE, stx["o4_free"])
                    m4 = 2 * pr + j
                    h = 4 * g + m4
                    tS.inc(TE.matmul(
                        O4[:, j * 512 + off:(j + 1) * 512],
                        vaug[:, kt * 1040 + h * 65: kt * 1040 + h * 65 + 65],
                        exp_sb[:, base:end],
                        start=(kt == 0), stop=(kt == last_kt)))
                    avj[(kt, j)] = tS.n
                exp_snap[kt] = sS.n
                av_snap[kt] = tS.n
                if kt + 2 < nkt:
                    emit_scores(kt + 2)
                if pace is not None:
                    pace()
            stx["av_prev"] = [av_snap[nkt - 2], av_snap[nkt - 1]]
            stx["sfree_s"][0] = exp_snap[nkt - 2]
            stx["sfree_s"][1] = exp_snap[nkt - 1]
            stx["exp_last"] = sS.n
            av_last = av_snap[last_kt]

            def emit_divide(g=g, pr=pr, w=w, av_last=av_last):
                if not stx.get("cdiv_ok"):
                    tS.wait(SE, stx["unfold_t"])
                    tS.wait(SY, stx["unfold_t"])
                    stx["cdiv_ok"] = True
                slot = stx.get("rb_slot", 0)
                stx["rb_slot"] = 1 - slot
                tS.wait(SE, av_last)
                vS.wait(SE, stx["o4_free"])
                sS.inc(SE.activation(cdiv[32:33, 0:1024], O4[64:65, 0:1024], AF.Ln))
                sS.wait(SE)
                r_s = sS.inc(SE.activation(cdiv[0:1, 0:1024], cdiv[32:33, 0:1024], AF.Exp,
                                           scale=cvec[0:1, 3:4]))
                sS.wait(SY, r_s)
                dbi.wait(SY)
                dbo.wait(SY)
                dbo.inc(SY.dma_start(rscr_d[slot].rearrange("(one n) -> one n", one=1),
                                     cdiv[0:1, 0:1024]), 16)
                vS.wait(SY, stx["o4_free"])
                dbo.wait(SY)
                bcast = rscr_d[slot].rearrange("(one n) -> one n", one=1).broadcast_to([64, 1024])
                bin_s = dbi.inc(SY.dma_start(cdiv[0:64, 0:1024], bcast), 16)
                dbi.wait(VE, bin_s)
                tS.wait(VE, stx["ot_r"])
                for j in range(2):
                    h = 4 * g + 2 * pr + j
                    pt, rr = h // 2, (h % 2) * 64
                    vS.inc(VE.tensor_tensor(
                        oT[rr:rr + 64, pt * 2048 + w * 512: pt * 2048 + w * 512 + 512],
                        O4[0:64, j * 512:j * 512 + 512],
                        cdiv[0:64, j * 512:j * 512 + 512], OP.mult))
                stx["o4_free"] = vS.n
                stx["odiv"] = vS.n

            stx["pending_div"] = emit_divide

        for tt in range(NT):
            A1_tile(0, tt)
            A2_tile(0, tt)
            for _ in range(2):
                if unf_steps:
                    unf_steps.popleft()()
        for ch in range(4):
            B1_chunk(0, ch)
            B2_chunk(0, ch, 0)
            B2_chunk(0, ch, 1)
        for g in range(4):
            V_group(0, g, 0)
        for tt in range(NT):
            FFN_pass1(0, tt)

        q1 = deque()
        q2 = deque()
        for b in range(nb):
            if b + 1 < nb:
                for tt in range(NT):
                    q1.append((0, lambda tt=tt, b1=b + 1: A1_tile(b1, tt)))
                    q1.append((1, lambda tt=tt, b1=b + 1: A2_tile(b1, tt)))
                for tt in range(NT):
                    q1.append((1, lambda tt=tt, b1=b + 1: FFN_pass1(b1, tt)))
                for ch in range(4):
                    q2.append((1, lambda ch=ch, b1=b + 1: B1_chunk(b1, ch)))
                    q2.append((1, lambda ch=ch, b1=b + 1: B2_chunk(b1, ch, 0)))
                    q2.append((1, lambda ch=ch, b1=b + 1: B2_chunk(b1, ch, 1)))
            state = {"step": 0, "gate": False, "popped": 0.0}
            total_items = sum(w for w, _ in q1) + sum(w for w, _ in q2) + (4 if b + 1 < nb else 0)
            def pace():
                state["step"] += 1
                target = state["step"] * total_items / 320.0
                while state["popped"] < target:
                    if q1:
                        w_, fn = q1.popleft()
                    elif state["gate"] and q2:
                        w_, fn = q2.popleft()
                    else:
                        break
                    fn()
                    state["popped"] += max(w_, 0.5)
            for g in range(4):
                C_scatter(g)
                if g == 3:
                    state["gate"] = True
                for pr in range(2):
                    for w in range(NW):
                        C_pair(b, g, pr, w, pace)
                if b + 1 < nb:
                    q2.append((1, lambda g=g, b1=b + 1, snap=tS.n: V_group(b1, g, snap)))
            if stx.get("pending_div") is not None:
                stx["pending_div"]()
                stx["pending_div"] = None
            while q1:
                q1.popleft()[1]()
            while q2:
                q2.popleft()[1]()
            burst_outproj(b)

        dyp.wait(SY)
        dya.wait(SY)
    return nc



def kernel(L0_attn_x, L0_attn_W1_q, L0_attn_W2_q, L0_attn_W1_k, L0_attn_W2_k,
           L0_attn_W_v, L0_attn_b_v, L0_attn_W_out, L0_attn_b_out,
           L0_ffn_G1_R, L0_ffn_G2_R, L0_ffn_G1_I, L0_ffn_G2_I,
           L0_ffn_bias_real, L0_ffn_bias_imag):
    x = np.ascontiguousarray(np.asarray(L0_attn_x, dtype=np.float32))
    pi = make_pi()
    Wqk = np.concatenate([np.asarray(L0_attn_W1_q), np.asarray(L0_attn_W2_q),
                          np.asarray(L0_attn_W1_k), np.asarray(L0_attn_W2_k)], axis=0).astype(np.float32)
    wqkT = np.ascontiguousarray(Wqk[:, pi].T)
    wvT = np.ascontiguousarray(np.asarray(L0_attn_W_v, dtype=np.float32)[:, pi].T)
    woutT = np.ascontiguousarray(np.asarray(L0_attn_W_out, dtype=np.float32).T)
    bv = np.ascontiguousarray(np.asarray(L0_attn_b_v, dtype=np.float32))
    bout = np.ascontiguousarray(np.asarray(L0_attn_b_out, dtype=np.float32))
    bri = np.concatenate([np.asarray(L0_ffn_bias_real), np.asarray(L0_ffn_bias_imag)]).astype(np.float32)
    g1a = np.stack([np.asarray(L0_ffn_G1_R, dtype=np.float32)[0].transpose(1, 0, 2).reshape(4, 1024),
                    np.asarray(L0_ffn_G1_I, dtype=np.float32)[0].transpose(1, 0, 2).reshape(4, 1024)])
    g2b = np.stack([np.asarray(L0_ffn_G2_R, dtype=np.float32)[:, :, 0, :].reshape(4, 256),
                    np.asarray(L0_ffn_G2_I, dtype=np.float32)[:, :, 0, :].reshape(4, 256)])
    g1a = np.ascontiguousarray(g1a); g2b = np.ascontiguousarray(g2b)
    consts = make_consts()

    nc = build(NB)
    shared = dict(wqkT=wqkT, wvT=wvT, woutT=woutT, bv=bv, bout=bout, bri=bri,
                  g1a=g1a, g2b=g2b,
                  c_idt=consts["idt"], c_tri=consts["tri"], c_perm=consts["perm"],
                  c_ssq=consts["ssq"], c_sbc=consts["sbc"], c_spad=consts["spad"],
                  c_ones64=consts["ones64"], c_cvec=consts["cvec"])
    in_maps = []
    for core in range(8):
        m = dict(shared)
        m["x"] = np.ascontiguousarray(x[core * NB:(core + 1) * NB])
        in_maps.append(m)
    res = run_bass_kernel_spmd(nc, in_maps, core_ids=list(range(8)))
    return np.concatenate([np.asarray(r["y"], dtype=np.float32) for r in res.results], axis=0)


# version 30 (final)
# speedup vs baseline: 1.0253x; 1.0253x over previous
import numpy as np
from contextlib import ExitStack
from collections import deque

import concourse.bass as bass
import concourse.mybir as mybir
from concourse.bass import AP
from concourse.bass_utils import run_bass_kernel_spmd

F32 = mybir.dt.float32
BF16 = mybir.dt.bfloat16
AF = mybir.ActivationFunctionType
OP = mybir.AluOpType
AX = mybir.AxisListType

NB = 4
T = 2048
NT = 16
NW = 4
D = 1024
H = 16
PAIRS = [(0, 1), (0, 2), (0, 3), (1, 2), (1, 3), (2, 3)]
SIG = [1.0, -1.0, 1.0, 1.0, -1.0, 1.0]
SCALE = 6.0 ** (-0.5)


def make_pi():
    pi = np.zeros(1024, dtype=np.int64)
    for p in range(1024):
        q = p % 512
        pi[p] = (p // 512) * 512 + (q % 32) * 16 + (q // 32)
    return pi


def make_consts():
    c = {}
    c["idt"] = np.eye(128, dtype=np.float32)
    c["tri"] = (np.arange(128)[None, :] >= np.arange(128)[:, None]).astype(np.float32)
    perm = np.zeros((8, 64, 96), dtype=np.float32)
    for side in range(2):
        for j in range(4):
            P = perm[side * 4 + j]
            for h in range(H):
                for cc in range(6):
                    p = 5 - cc if side == 1 else cc
                    i_, j_ = PAIRS[p]
                    comp = [i_, j_, j_, i_][j]
                    P[4 * h + comp, 6 * h + cc] = 1.0
    packed = np.zeros((4, 128, 96), dtype=np.float32)
    for s in range(2):
        packed[s * 2 + 0, 0:64] = perm[s * 4 + 0]
        packed[s * 2 + 1, 0:64] = perm[s * 4 + 2]
        packed[s * 2 + 0, 64:128] = perm[s * 4 + 1]
        packed[s * 2 + 1, 64:128] = perm[s * 4 + 3]
    c["perm"] = packed
    ssq = np.zeros((96, 16), dtype=np.float32)
    for h in range(H):
        ssq[6 * h:6 * h + 6, h] = 1.0
    c["ssq"] = ssq
    sbc = np.zeros((2, 16, 96), dtype=np.float32)
    for h in range(H):
        for cc in range(6):
            sbc[0, h, 6 * h + cc] = SIG[cc] * SCALE
            sbc[1, h, 6 * h + cc] = 1.0
    c["sbc"] = sbc
    spad = np.zeros((4, 96, 128), dtype=np.float32)
    for g in range(4):
        for m in range(4):
            for cc in range(6):
                spad[g, 6 * (4 * g + m) + cc, 32 * m + cc] = 1.0
    c["spad"] = spad
    c["ones64"] = np.ones((1, 64), dtype=np.float32)
    c["cvec"] = np.tile(np.array([1e-5, 1e-24, -0.5, -1.0], dtype=np.float32), (128, 1))
    return c


class Sem:
    def __init__(self, stack, nc, name):
        self.h = stack.enter_context(nc.semaphore(name))
        self.n = 0
    def inc(self, instr, k=1):
        instr.then_inc(self.h, k)
        self.n += k
        return self.n
    def wait(self, eng, n=None):
        eng.wait_ge(self.h, self.n if n is None else n)


def mkap(sl, extra_offset, free_dims, nparts=None):
    p = list(sl.ap[0])
    if nparts is not None:
        p[1] = nparts
    return AP(sl.tensor, sl.offset + extra_offset, [p] + [list(d) for d in free_dims])


def pi_out_ap(tens):
    return mkap(tens, 0, [[512, 2], [1, 32], [32, 16]])


def build(nb=NB, debug=False):
    nc = bass.Bass("TRN2", target_bir_lowering=False)
    def din(name, shape, dtype=F32):
        return nc.dram_tensor(name, list(shape), dtype, kind="ExternalInput").ap()

    x_d = din("x", [nb, T, D])
    wqkT_d = din("wqkT", [D, 256])
    wvT_d = din("wvT", [D, D])
    woutT_d = din("woutT", [D, D])
    bv_d = din("bv", [D])
    bout_d = din("bout", [D])
    bri_d = din("bri", [D])
    g1a_d = din("g1a", [2, 4, 1024])
    g2b_d = din("g2b", [2, 4, 256])
    idt_d = din("c_idt", [128, 128])
    tri_d = din("c_tri", [128, 128])
    perm_d = din("c_perm", [4, 128, 96])
    ssq_d = din("c_ssq", [96, 16])
    sbc_d = din("c_sbc", [2, 16, 96])
    spad_d = din("c_spad", [4, 96, 128])
    ones64_d = din("c_ones64", [1, 64])
    cvec_d = din("c_cvec", [128, 4])

    y_d = nc.dram_tensor("y", [nb, T, D], F32, kind="ExternalOutput").ap()
    tascr_d = nc.dram_tensor("tascr", [2, 1024, 256], F32, kind="Internal").ap()
    rscr_d = nc.dram_tensor("rscr", [2, 1024], F32, kind="Internal").ap()

    st = ExitStack()
    sb = lambda name, shape, dtype: st.enter_context(nc.sbuf_tensor(name, shape, dtype))
    with st:
        xnT = sb("xnT", [128, 8 * 2048], BF16)
        vaug = sb("vaug", [128, 16 * 1040], BF16)
        oT = sb("oT", [128, 8 * 2048], BF16)
        wqk_sb = sb("wqk", [128, 8 * 256], BF16)
        wv_sb = sb("wv", [128, 8 * 1024], BF16)
        wout_sb = sb("wout", [128, 8 * 1024], BF16)
        wr_sb = sb("wr", [128, 4 * 512], BF16)
        wi_sb = sb("wi", [128, 4 * 512], BF16)
        negscr = sb("negscr", [128, 512], BF16)
        bv_bc = sb("bvbc", [128, 1024], BF16)
        ball_bc = sb("ballbc", [128, 1024], F32)
        idt_sb = sb("idt", [128, 128], F32)
        tri_sb = sb("tri", [128, 128], BF16)
        perm_sb = sb("perm", [128, 4 * 96], F32)
        ssq_sb = sb("ssq", [96, 16], F32)
        sbc_sb = sb("sbc", [16, 2 * 96], F32)
        spad_sb = sb("spad", [96, 4 * 128], BF16)

        cvec = sb("cvec", [128, 4], F32)
        QK_sb = sb("qk", [128, 1024], F32)
        SCR = sb("scr", [128, 1536], F32)
        LqN = sb("lqn", [128, 2048], BF16)
        LkN = sb("lkn", [128, 2048], BF16)
        fq_pad = sb("fqp", [128, 2048], BF16)
        fk_pad = sb("fkp", [128, 2048], BF16)
        exp_sb = sb("exps", [128, 2048], BF16)
        xstage = sb("xst", [128, 1024], F32)
        xnstage = sb("xnst", [128, 1024], F32)
        x2stage = sb("x2st", [128, 1024], F32)
        ostage = sb("ost", [128, 2 * 1024], F32)
        cdiv = sb("cdiv", [64, 1024], F32)
        bscr = sb("bscr", [16, 512], F32)
        stat = sb("stat", [128, 8], F32)

        S_ps = st.enter_context(nc.psum_tensor("S", [128, 2048], F32))
        O4 = st.enter_context(nc.psum_tensor("O4", [65, 1024], F32))
        F6 = st.enter_context(nc.psum_tensor("F6", [128, 512], F32))
        F7 = st.enter_context(nc.psum_tensor("F7", [128, 512], F32))
        FB = [F6, F7]

        sems = {}
        for name in ["dwp", "dxp", "dws", "dyt", "dx", "dx2", "dyp", "dya", "dbo", "dbi", "dtr", "t", "v", "s"]:
            sems[name] = Sem(st, nc, "sem_" + name)
        dwp, dxp, dws, dyt = sems["dwp"], sems["dxp"], sems["dws"], sems["dyt"]
        dx, dx2, dyp, dya = sems["dx"], sems["dx2"], sems["dyp"], sems["dya"]
        dbo, dbi = sems["dbo"], sems["dbi"]
        dtr = sems["dtr"]
        tS, vS, sS = sems["t"], sems["v"], sems["s"]
        TE, VE, SE, GE, SY = nc.tensor, nc.vector, nc.scalar, nc.gpsimd, nc.sync

        stx = {
            "xst_rs": [0, 0],
            "x2_r": 0,
            "xn_r": 0,
            "f_w": [0, 0],
            "f_r": [0, 0],
            "f_rs": [0, 0],
            "qk_w": 0,
            "xnT_tile": [0] * NT,
            "lq_done": 0,
            "vaug_grp": [0] * 4,
            "sfree_s": [0, 0],
            "sfree_v": [0, 0],
            "o4_free": 0,
            "ot_r": 0,
            "odiv": 0,
            "exp_last": 0,
            "av_prev": [0, 0],
            "ost_rp": [0, 0],
            "ost_ra": [0, 0],
            "dyp_last": 0, "dya_last": 0,
            "unfold_t": 0,
        }

        gloads = [(tri_sb[:], tri_d),
                  (bv_bc[:], bv_d.rearrange("(one d) -> one d", one=1).broadcast_to([128, 1024]))]
        for kc in range(8):
            gloads.append((wqk_sb[:, kc * 256:(kc + 1) * 256], wqkT_d[kc * 128:(kc + 1) * 128, :]))
            gloads.append((wv_sb[:, kc * 1024:(kc + 1) * 1024], wvT_d[kc * 128:(kc + 1) * 128, :]))
            gloads.append((wout_sb[:, kc * 1024:(kc + 1) * 1024], woutT_d[kc * 128:(kc + 1) * 128, :]))
        for g in range(4):
            gloads.append((spad_sb[:, g * 128:(g + 1) * 128], spad_d[g]))
        for out_ap, in_ap in gloads:
            dwp.inc(GE.dma_start(out_ap, in_ap), 16)
        sloads = [
            (idt_sb[:], idt_d),
            (ssq_sb[:], ssq_d),
            (cvec[:], cvec_d),
            (ostage[:, 0:1024], bout_d.rearrange("(one d) -> one d", one=1).broadcast_to([128, 1024])),
            (ostage[:, 1024:2048], bri_d.rearrange("(one d) -> one d", one=1).broadcast_to([128, 1024])),
            (SCR[0:4, 0:1024], g1a_d[0]),
            (cdiv[0:4, 0:1024], g1a_d[1]),
            (QK_sb[0:4, 512:768], g2b_d[0]),
            (QK_sb[0:4, 768:1024], g2b_d[1]),
        ]
        for s8 in range(4):
            sloads.append((perm_sb[:, s8 * 96:(s8 + 1) * 96], perm_d[s8]))
        for s2 in range(2):
            sloads.append((sbc_sb[:, s2 * 96:(s2 + 1) * 96], sbc_d[s2]))
        for out_ap, in_ap in sloads:
            dxp.inc(SY.dma_start(out_ap, in_ap), 16)
        dxp.wait(VE)
        vS.inc(VE.tensor_tensor(ball_bc[:], ostage[:, 0:1024], ostage[:, 1024:2048], OP.add))
        dxp.wait(TE)
        dxp.wait(SE)

        def unf_step(ri, mt):
            half = mt % 2
            Fb = FB[half]
            qp = QK_sb[:, half * 256:(half + 1) * 256]
            dsem = dyt if half == 0 else dtr
            vS.wait(TE, stx["f_r"][half])
            sS.wait(TE, stx["f_rs"][half])
            src_g = SCR if ri == 0 else cdiv
            tS.inc(TE.matmul(Fb[:, 0:256],
                             src_g[0:4, mt * 128:(mt + 1) * 128],
                             QK_sb[0:4, 512 + ri * 256: 768 + ri * 256],
                             start=True, stop=True))
            stx["f_w"][half] = tS.n
            stx["unfold_t"] = tS.n
            tS.wait(VE, stx["f_w"][half])
            dsem.wait(VE)
            stx["f_r"][half] = vS.inc(VE.tensor_copy(qp, mkap(Fb[:, 0:256], 0, [[1, 16], [16, 16]])))
            vS.wait(GE, stx["f_r"][half])
            dsem.inc(GE.dma_start(tascr_d[ri, mt * 128:(mt + 1) * 128, :], qp), 16)

        def unf_fin():
            dyt.wait(GE)
            dtr.wait(GE)
            for ri, wdst in [(0, wr_sb), (1, wi_sb)]:
                base = tascr_d[ri]
                for g in range(16):
                    in_ap = AP(base.tensor, base.offset + g * 16, [[256, 32], [8192, 32], [1, 16]])
                    out_sl = wdst[(g % 4) * 32:(g % 4) * 32 + 32,
                                  (g // 4) * 512:(g // 4) * 512 + 512]
                    dws.inc(GE.dma_start(mkap(out_sl, 0, [[16, 32], [1, 16]]), in_ap), 16)

        unf_steps = deque([(lambda ri=ri, mt=mt: unf_step(ri, mt))
                           for ri in range(2) for mt in range(8)] + [unf_fin])

        def A1_tile(b1, tt, slot=0):
            xs = xstage[:] if slot == 0 else exp_sb[:, 0:2048].bitcast(F32)
            dxs = dx if slot == 0 else dx2
            vS.wait(SY, stx["xst_rs"][slot])
            dxs.inc(SY.dma_start(xs, x_d[b1, tt * 128:(tt + 1) * 128, :]), 16)
            dxs.wait(VE)
            if slot == 0:
                dx2.wait(VE)
            vS.inc(VE.tensor_tensor(x2stage[:], xs, xs, OP.mult))
            vS.inc(VE.reduce_sum(stat[:, 0:1], xs, axis=AX.X))
            vS.wait(VE)
            vS.inc(VE.reduce_sum(stat[:, 1:2], x2stage[:], axis=AX.X))
            stx["x2_r"] = vS.n
            vS.inc(VE.tensor_scalar(stat[:, 2:3], stat[:, 0:1], 1.0 / 1024, None, OP.mult))
            vS.wait(VE)
            vS.inc(VE.tensor_scalar(stat[:, 3:4], stat[:, 1:2], 1.0 / 1024, None, OP.mult))
            vS.inc(VE.tensor_tensor(stat[:, 4:5], stat[:, 2:3], stat[:, 2:3], OP.mult))
            vS.wait(VE)
            var_s = vS.inc(VE.tensor_tensor(stat[:, 5:6], stat[:, 3:4], stat[:, 4:5], OP.subtract))
            vS.wait(SE, var_s)
            sS.inc(SE.activation(stat[:, 6:7], stat[:, 5:6], AF.Ln, bias=cvec[:, 0:1]))
            sS.wait(SE)
            rstd_s = sS.inc(SE.activation(stat[:, 7:8], stat[:, 6:7], AF.Exp, scale=cvec[:, 2:3]))
            sS.wait(VE, rstd_s)
            vS.wait(VE)
            tS.wait(VE, stx.get("xn_r", 0))
            dx2.wait(VE)
            xn_s = vS.inc(VE.tensor_scalar(pi_out_ap(xnstage[:]), xs, stat[:, 2:3],
                                           stat[:, 7:8], OP.subtract, OP.mult))
            stx["xst_rs"][slot] = xn_s
            stx["a_xn"] = xn_s

        def A2_tile(b1, tt):
            xn_s = stx["a_xn"]
            for half in range(2):
                Fb = FB[half]
                vS.wait(TE, max(xn_s, stx["f_r"][half]))
                sS.wait(TE, stx["f_rs"][half])
                for c4 in range(4):
                    c = half * 4 + c4
                    tS.inc(TE.transpose(Fb[:, c4 * 128:(c4 + 1) * 128],
                                        xnstage[:, c * 128:(c + 1) * 128], idt_sb[:]))
                stx["f_w"][half] = tS.n
                tS.wait(VE, stx["f_w"][half])
                out_ap = mkap(xnT[:], (half * 4) * 2048 + tt * 128, [[2048, 4], [1, 128]])
                stx["f_r"][half] = vS.inc(VE.tensor_copy(out_ap, Fb[:, 0:512]))
            stx["xn_r"] = tS.n
            stx["xnT_tile"][tt] = stx["f_r"][1]

        def B1_chunk(b1, ch):
            if not stx.get("gl_ok"):
                dwp.wait(TE)
                dwp.wait(VE)
                dyt.wait(VE)
                dtr.wait(VE)
                stx["gl_ok"] = True
            vS.wait(TE, max(stx["xnT_tile"][4 * ch + 3], stx["f_r"][0], stx["f_r"][1]))
            sS.wait(TE, max(stx["f_rs"][0], stx["f_rs"][1]))
            for mt in range(2):
                for kc in range(8):
                    ins = TE.matmul(FB[mt][:, 0:512],
                                    wqk_sb[:, kc * 256 + mt * 128: kc * 256 + mt * 128 + 128],
                                    xnT[:, kc * 2048 + ch * 512: kc * 2048 + ch * 512 + 512],
                                    start=(kc == 0), stop=(kc == 7))
                tS.inc(ins)
                stx["f_w"][mt] = tS.n
            for mt in range(2):
                tS.wait(VE, stx["f_w"][mt])
                stx["f_r"][mt] = vS.inc(VE.tensor_copy(QK_sb[:, mt * 512:(mt + 1) * 512],
                                                       FB[mt][:, 0:512]))
            stx["qk_w"] = vS.n

        def B2_chunk(b1, ch, s):
            c0, c1, c2 = (SCR[0:96, i * 512:(i + 1) * 512] for i in range(3))
            c3 = c2
            if True:
                a1 = QK_sb[0:64, s * 512:s * 512 + 512]
                a2 = QK_sb[64:128, s * 512:s * 512 + 512]
                vS.wait(TE, max(stx["qk_w"], stx["f_r"][0], stx["f_r"][1]))
                sS.wait(TE, max(stx["f_rs"][0], stx["f_rs"][1]))
                pU = perm_sb[0:64, (s * 2 + 0) * 96:(s * 2 + 0) * 96 + 96]
                pV = perm_sb[64:128, (s * 2 + 0) * 96:(s * 2 + 0) * 96 + 96]
                w6 = tS.inc(TE.matmul(F6[0:96, 0:512], pU, a1, start=True, stop=True))
                w7 = tS.inc(TE.matmul(F7[0:96, 0:512], pV, a2, start=True, stop=True))
                tS.wait(VE, w6)
                vS.inc(VE.tensor_copy(c0, F6[0:96, 0:512]))
                stx["f_r"][0] = vS.n
                tS.wait(VE, w7)
                vS.wait(VE)
                vS.inc(VE.tensor_tensor(c1, c0, F7[0:96, 0:512], OP.mult))
                stx["f_r"][1] = vS.n
                vS.wait(TE, max(stx["f_r"][0], stx["f_r"][1]))
                pU2 = perm_sb[0:64, (s * 2 + 1) * 96:(s * 2 + 1) * 96 + 96]
                pV2 = perm_sb[64:128, (s * 2 + 1) * 96:(s * 2 + 1) * 96 + 96]
                w6 = tS.inc(TE.matmul(F6[0:96, 0:512], pU2, a1, start=True, stop=True))
                w7 = tS.inc(TE.matmul(F7[0:96, 0:512], pV2, a2, start=True, stop=True))
                tS.wait(VE, w6)
                vS.inc(VE.tensor_copy(c2, F6[0:96, 0:512]))
                stx["f_r"][0] = vS.n
                tS.wait(VE, w7)
                vS.wait(VE)
                vS.inc(VE.tensor_tensor(c0, c2, F7[0:96, 0:512], OP.mult))
                stx["f_r"][1] = vS.n
                vS.wait(VE)
                vS.inc(VE.tensor_tensor(c3, c1, c0, OP.subtract))
                vS.wait(VE)
                l2_s = vS.inc(VE.tensor_tensor(c0, c3, c3, OP.mult))
                vS.wait(TE, max(l2_s, stx["f_r"][0]))
                n2_s = tS.inc(TE.matmul(F6[0:16, 0:512], ssq_sb[:], c0, start=True, stop=True))
                tS.wait(SE, n2_s)
                sS.inc(SE.activation(SCR[96:112, 0:512], F6[0:16, 0:512], AF.Ln, bias=cvec[96:112, 1:2]))
                stx["f_rs"][0] = sS.n
                sS.wait(SE)
                rn_s = sS.inc(SE.activation(bscr[:], SCR[96:112, 0:512], AF.Exp,
                                            scale=cvec[96:112, 2:3]))
                sS.wait(TE, rn_s)
                vS.wait(TE, stx["f_r"][1])
                bcn_s = tS.inc(TE.matmul(F7[0:96, 0:512], sbc_sb[:, s * 96:s * 96 + 96],
                                         bscr[:], start=True, stop=True))
                tS.wait(VE, bcn_s)
                vS.wait(VE)
                dst = (LqN if s == 0 else LkN)
                stx["f_r"][1] = vS.inc(VE.tensor_tensor(dst[0:96, ch * 512:ch * 512 + 512],
                                                        c3, F7[0:96, 0:512], OP.mult))
            if ch == 3 and s == 1:
                stx["lq_done"] = vS.n

        def V_group(b1, g, cav_snap):
            for tt in range(NT):
                p = tt % 2
                Fb = FB[p]
                vS.wait(TE, max(stx["f_r"][p], stx["xnT_tile"][tt]))
                sS.wait(TE, stx["f_rs"][p])
                for kc in range(8):
                    ins = TE.matmul(Fb[:, 0:256],
                                    xnT[:, kc * 2048 + tt * 128: kc * 2048 + tt * 128 + 128],
                                    wv_sb[:, kc * 1024 + g * 256: kc * 1024 + g * 256 + 256],
                                    start=(kc == 0), stop=(kc == 7))
                tS.inc(ins)
                stx["f_w"][p] = tS.n
                tS.wait(VE, max(stx["f_w"][p], cav_snap))
                base = tt * 1040 + (4 * g) * 65
                vaug_ap = mkap(vaug[:], base, [[65, 4], [1, 64]])
                bvs = mkap(bv_bc[:], g * 256, [[64, 4], [1, 64]])
                vS.inc(VE.tensor_tensor(vaug_ap, Fb[:, 0:256], bvs, OP.add))
                ones_ap = mkap(vaug[:], base + 64, [[65, 4]])
                stx["f_r"][p] = vS.inc(VE.memset(ones_ap, 1.0))
            stx["vaug_grp"][g] = vS.n

        def FFN_pass1(b1, tt):
            if not stx.get("w_ok"):
                dws.wait(TE)
                stx["w_ok"] = True
            vS.wait(SY, stx["x2_r"])
            dx2.inc(SY.dma_start(x2stage[:], x_d[b1, tt * 128:(tt + 1) * 128, :]), 16)
            dx2s = dx2.n
            vS.wait(TE, max(stx["f_r"][0], stx["f_r"][1], stx["xnT_tile"][tt]))
            sS.wait(TE, max(stx["f_rs"][0], stx["f_rs"][1]))
            vS.wait(VE, stx["xnT_tile"][tt])
            tS.wait(VE, stx.get("neg_r", 0))
            for kc in range(4):
                vS.inc(VE.tensor_scalar(negscr[:, kc * 128:(kc + 1) * 128],
                                        xnT[:, (kc + 4) * 2048 + tt * 128: (kc + 4) * 2048 + tt * 128 + 128],
                                        -1.0, None, OP.mult))
            neg_s = vS.n
            vS.wait(TE, neg_s)
            for kc in range(8):
                rhs = (wr_sb if kc < 4 else wi_sb)[:, (kc % 4) * 512:(kc % 4) * 512 + 512]
                lhsT = (xnT[:, kc * 2048 + tt * 128: kc * 2048 + tt * 128 + 128] if kc < 4
                        else negscr[:, (kc - 4) * 128:(kc - 3) * 128])
                ins = TE.matmul(F6[:, 0:512], lhsT,
                                rhs, start=(kc == 0), stop=(kc == 7))
            tS.inc(ins)
            stx["f_w"][0] = tS.n
            stx["neg_r"] = tS.n
            for kc in range(8):
                rhs = (wi_sb if kc < 4 else wr_sb)[:, (kc % 4) * 512:(kc % 4) * 512 + 512]
                ins = TE.matmul(F7[:, 0:512],
                                xnT[:, kc * 2048 + tt * 128: kc * 2048 + tt * 128 + 128],
                                rhs, start=(kc == 0), stop=(kc == 7))
            tS.inc(ins)
            stx["f_w"][1] = tS.n
            tS.wait(VE, stx["f_w"][0])
            dyp.wait(VE, stx["ost_rp"][0])
            dya.wait(VE, stx["ost_ra"][0])
            vS.inc(VE.tensor_tensor(ostage[:, 0:512], F6[:, 0:512], ball_bc[:, 0:512], OP.add))
            stx["f_r"][0] = vS.n
            tS.wait(VE, stx["f_w"][1])
            vS.inc(VE.tensor_tensor(ostage[:, 512:1024], F7[:, 0:512], ball_bc[:, 512:1024], OP.add))
            stx["f_r"][1] = vS.n
            dx2.wait(VE, dx2s)
            dyp.wait(VE, stx["ost_rp"][1])
            dya.wait(VE, stx["ost_ra"][1])
            vS.wait(VE)
            o2_s = vS.inc(VE.tensor_tensor(ostage[:, 1024:2048], ostage[:, 0:1024], x2stage[:], OP.add))
            stx["x2_r"] = o2_s
            vS.wait(GE, o2_s)
            dyp.wait(GE, stx["dyp_last"])
            stx["dyp_last"] = dyp.inc(GE.dma_start(y_d[b1, tt * 128:(tt + 1) * 128, :],
                                                   ostage[:, 1024:2048]), 16)
            stx["ost_rp"][1] = stx["dyp_last"]

        def OutProjPart(b, g, tt):
            p = tt % 2
            vS.wait(TE, max(stx["f_r"][0], stx["f_r"][1], stx["odiv"]))
            sS.wait(TE, max(stx["f_rs"][0], stx["f_rs"][1]))
            for half in range(2):
                Fb = FB[half]
                for k2 in range(2):
                    pt = 2 * g + k2
                    ins = TE.matmul(Fb[:, 0:512],
                                    oT[:, pt * 2048 + tt * 128: pt * 2048 + tt * 128 + 128],
                                    wout_sb[:, pt * 1024 + half * 512: pt * 1024 + half * 512 + 512],
                                    start=(k2 == 0), stop=(k2 == 1))
                tS.inc(ins)
                stx["f_w"][half] = tS.n
            for half in range(2):
                tS.wait(VE, stx["f_w"][half])
                if half == 0:
                    dyp.wait(VE, stx["ost_rp"][p])
                    dya.wait(VE, stx["ost_ra"][p])
                stx["f_r"][half] = vS.inc(VE.tensor_copy(
                    ostage[:, p * 1024 + half * 512: p * 1024 + half * 512 + 512], FB[half][:, 0:512]))
            cp_s = vS.n
            vS.wait(GE, cp_s)
            dya.wait(GE, stx["dya_last"])
            stx["dya_last"] = dya.inc(GE.dma_start(y_d[b, tt * 128:(tt + 1) * 128, :],
                                                   ostage[:, p * 1024:(p + 1) * 1024],
                                                   accum_op=OP.add), 16)
            stx["ost_ra"][p] = stx["dya_last"]

        def burst_outproj(b):
            for tt in range(NT):
                p = tt % 2
                Pp = S_ps[:, p * 1024:(p + 1) * 1024]
                sS.wait(TE, stx["exp_last"])
                vS.wait(TE, max(stx["odiv"], stx["sfree_v"][p]))
                for half in range(2):
                    for pt in range(8):
                        ins = TE.matmul(Pp[:, half * 512:half * 512 + 512],
                                        oT[:, pt * 2048 + tt * 128: pt * 2048 + tt * 128 + 128],
                                        wout_sb[:, pt * 1024 + half * 512: pt * 1024 + half * 512 + 512],
                                        start=(pt == 0), stop=(pt == 7))
                tS.inc(ins)
                pw = tS.n
                tS.wait(VE, pw)
                dyp.wait(VE, stx["ost_rp"][p])
                dya.wait(VE, stx["ost_ra"][p])
                cp_s = vS.inc(VE.tensor_copy(ostage[:, p * 1024:(p + 1) * 1024], Pp))
                stx["sfree_v"][p] = cp_s
                vS.wait(GE, cp_s)
                dya.wait(GE, stx["dya_last"])
                stx["dya_last"] = dya.inc(GE.dma_start(y_d[b, tt * 128:(tt + 1) * 128, :],
                                                       ostage[:, p * 1024:(p + 1) * 1024],
                                                       accum_op=OP.add), 16)
                stx["ost_ra"][p] = stx["dya_last"]
            stx["ot_r"] = tS.n

        def C_scatter(g):
            for s, (srcF, dstp) in enumerate([(LqN, fq_pad), (LkN, fk_pad)]):
                for ch in range(4):
                    pset = (ch + 4 * s) % 2
                    sS.wait(TE, stx["sfree_s"][pset])
                    vS.wait(TE, max(stx["lq_done"], stx["sfree_v"][pset]))
                    w_s = tS.inc(TE.matmul(S_ps[:, pset * 1024:pset * 1024 + 512],
                                           spad_sb[:, g * 128:(g + 1) * 128],
                                           srcF[0:96, ch * 512:ch * 512 + 512],
                                           start=True, stop=True))
                    tS.wait(VE, w_s)
                    stx["sfree_v"][pset] = vS.inc(
                        VE.tensor_copy(dstp[:, ch * 512:ch * 512 + 512],
                                       S_ps[:, pset * 1024:pset * 1024 + 512]))

        def C_pair(b, g, pr, w, pace=None):
            if not stx.get("xbuf_ok"):
                vS.wait(SE, stx["xst_rs"][1])
                stx["xbuf_ok"] = True
            last_kt = 4 * w + 3
            nkt = 4 * w + 4
            exp_snap = {}
            av_snap = {-2: stx["av_prev"][0], -1: stx["av_prev"][1]}
            sc_snap = {}

            def emit_scores(kt):
                ss = kt % 2
                soff = max(kt * 128 - w * 512, 0)
                if kt >= 2:
                    sS.wait(TE, exp_snap[kt - 2])
                else:
                    sS.wait(TE, stx["sfree_s"][ss])
                    vS.wait(TE, max(stx["sfree_v"][ss], stx["vaug_grp"][g]))
                for j in range(2):
                    m4 = 2 * pr + j
                    tS.inc(TE.matmul(
                        S_ps[:, ss * 1024 + j * 512 + soff: ss * 1024 + j * 512 + 512],
                        fk_pad[32 * m4:32 * m4 + 6, kt * 128:kt * 128 + 128],
                        fq_pad[32 * m4:32 * m4 + 6, w * 512 + soff:w * 512 + 512],
                        start=True, stop=True, tile_position=(32 * m4, 0)))
                sc_snap[kt] = tS.n

            emit_scores(0)
            if nkt > 1:
                emit_scores(1)
            if stx.get("pending_div") is not None:
                stx["pending_div"]()
                stx["pending_div"] = None
            for kt in range(nkt):
                off = max(kt * 128 - w * 512, 0)
                ss = kt % 2
                tS.wait(SE, max(av_snap[kt - 2], sc_snap[kt]))
                if off:
                    e_out = mkap(exp_sb[:], ss * 1024 + off, [[512, 2], [1, 512 - off]])
                    e_in = mkap(S_ps[:], ss * 1024 + off, [[512, 2], [1, 512 - off]])
                else:
                    e_out = exp_sb[:, ss * 1024: ss * 1024 + 1024]
                    e_in = S_ps[:, ss * 1024: ss * 1024 + 1024]
                sS.inc(SE.activation(e_out, e_in, AF.Exp))
                exp_snap[kt] = sS.n
                if kt + 2 < nkt:
                    emit_scores(kt + 2)
                if kt >= 4 * w:
                    sS.wait(VE, exp_snap[kt])
                    in0 = mkap(exp_sb[:], ss * 1024 + off, [[512, 2], [1, 128]])
                    tri_b = mkap(tri_sb[:], 0, [[0, 2], [1, 128]])
                    mk = vS.inc(VE.tensor_tensor(in0, in0, tri_b, OP.mult))
                    vS.wait(TE, mk)
                elif kt + 2 >= nkt:
                    sS.wait(TE, exp_snap[kt])
                if kt == 0:
                    vS.wait(TE, stx["o4_free"])
                for j in range(2):
                    m4 = 2 * pr + j
                    h = 4 * g + m4
                    ins = TE.matmul(
                        O4[:, j * 512 + off:(j + 1) * 512],
                        vaug[:, kt * 1040 + h * 65: kt * 1040 + h * 65 + 65],
                        exp_sb[:, ss * 1024 + j * 512 + off: ss * 1024 + (j + 1) * 512],
                        start=(kt == 0), stop=(kt == last_kt))
                tS.inc(ins)
                av_snap[kt] = tS.n
                if pace is not None:
                    pace()
            stx["av_prev"] = [av_snap[nkt - 2], av_snap[nkt - 1]]
            stx["sfree_s"][0] = exp_snap[nkt - 2]
            stx["sfree_s"][1] = exp_snap[nkt - 1]
            stx["exp_last"] = sS.n
            av_last = av_snap[last_kt]

            def emit_divide(g=g, pr=pr, w=w, av_last=av_last):
                if not stx.get("cdiv_ok"):
                    tS.wait(SE, stx["unfold_t"])
                    tS.wait(SY, stx["unfold_t"])
                    stx["cdiv_ok"] = True
                slot = stx.get("rb_slot", 0)
                stx["rb_slot"] = 1 - slot
                tS.wait(SE, av_last)
                vS.wait(SE, stx["o4_free"])
                sS.inc(SE.activation(cdiv[32:33, 0:1024], O4[64:65, 0:1024], AF.Ln))
                sS.wait(SE)
                r_s = sS.inc(SE.activation(cdiv[0:1, 0:1024], cdiv[32:33, 0:1024], AF.Exp,
                                           scale=cvec[0:1, 3:4]))
                sS.wait(SY, r_s)
                dbi.wait(SY)
                dbo.wait(SY)
                dbo.inc(SY.dma_start(rscr_d[slot].rearrange("(one n) -> one n", one=1),
                                     cdiv[0:1, 0:1024]), 16)
                vS.wait(SY, stx["o4_free"])
                dbo.wait(SY)
                bcast = rscr_d[slot].rearrange("(one n) -> one n", one=1).broadcast_to([64, 1024])
                bin_s = dbi.inc(SY.dma_start(cdiv[0:64, 0:1024], bcast), 16)
                dbi.wait(VE, bin_s)
                tS.wait(VE, stx["ot_r"])
                for j in range(2):
                    h = 4 * g + 2 * pr + j
                    pt, rr = h // 2, (h % 2) * 64
                    vS.inc(VE.tensor_tensor(
                        oT[rr:rr + 64, pt * 2048 + w * 512: pt * 2048 + w * 512 + 512],
                        O4[0:64, j * 512:j * 512 + 512],
                        cdiv[0:64, j * 512:j * 512 + 512], OP.mult))
                stx["o4_free"] = vS.n
                stx["odiv"] = vS.n

            stx["pending_div"] = emit_divide

        for tt in range(NT):
            A1_tile(0, tt, tt % 2)
            A2_tile(0, tt)
            for _ in range(2):
                if unf_steps:
                    unf_steps.popleft()()
        for ch in range(4):
            B1_chunk(0, ch)
            B2_chunk(0, ch, 0)
            B2_chunk(0, ch, 1)
        for g in range(4):
            V_group(0, g, 0)
        for tt in range(NT):
            FFN_pass1(0, tt)

        q1 = deque()
        q2 = deque()
        for b in range(nb):
            if b + 1 < nb:
                for tt in range(NT):
                    q1.append((0, lambda tt=tt, b1=b + 1: A1_tile(b1, tt)))
                    q1.append((1, lambda tt=tt, b1=b + 1: A2_tile(b1, tt)))
                for tt in range(NT):
                    q1.append((1, lambda tt=tt, b1=b + 1: FFN_pass1(b1, tt)))
                for ch in range(4):
                    q2.append((1, lambda ch=ch, b1=b + 1: B1_chunk(b1, ch)))
                    q2.append((1, lambda ch=ch, b1=b + 1: B2_chunk(b1, ch, 0)))
                    q2.append((1, lambda ch=ch, b1=b + 1: B2_chunk(b1, ch, 1)))
            state = {"step": 0, "gate": False, "popped": 0.0}
            total_items = sum(w for w, _ in q1) + sum(w for w, _ in q2) + (4 if b + 1 < nb else 0)
            def pace():
                state["step"] += 1
                target = state["step"] * total_items / 320.0
                while state["popped"] < target:
                    if q1:
                        w_, fn = q1.popleft()
                    elif state["gate"] and q2:
                        w_, fn = q2.popleft()
                    else:
                        break
                    fn()
                    state["popped"] += max(w_, 0.5)
            for g in range(4):
                C_scatter(g)
                if g == 3:
                    state["gate"] = True
                for pr in range(2):
                    for w in range(NW):
                        C_pair(b, g, pr, w, pace)
                if b + 1 < nb:
                    q2.append((1, lambda g=g, b1=b + 1, snap=tS.n: V_group(b1, g, snap)))
            if stx.get("pending_div") is not None:
                stx["pending_div"]()
                stx["pending_div"] = None
            while q1:
                q1.popleft()[1]()
            while q2:
                q2.popleft()[1]()
            burst_outproj(b)

        dyp.wait(SY)
        dya.wait(SY)
    return nc



def kernel(L0_attn_x, L0_attn_W1_q, L0_attn_W2_q, L0_attn_W1_k, L0_attn_W2_k,
           L0_attn_W_v, L0_attn_b_v, L0_attn_W_out, L0_attn_b_out,
           L0_ffn_G1_R, L0_ffn_G2_R, L0_ffn_G1_I, L0_ffn_G2_I,
           L0_ffn_bias_real, L0_ffn_bias_imag):
    x = np.ascontiguousarray(np.asarray(L0_attn_x, dtype=np.float32))
    pi = make_pi()
    Wqk = np.concatenate([np.asarray(L0_attn_W1_q), np.asarray(L0_attn_W2_q),
                          np.asarray(L0_attn_W1_k), np.asarray(L0_attn_W2_k)], axis=0).astype(np.float32)
    wqkT = np.ascontiguousarray(Wqk[:, pi].T)
    wvT = np.ascontiguousarray(np.asarray(L0_attn_W_v, dtype=np.float32)[:, pi].T)
    woutT = np.ascontiguousarray(np.asarray(L0_attn_W_out, dtype=np.float32).T)
    bv = np.ascontiguousarray(np.asarray(L0_attn_b_v, dtype=np.float32))
    bout = np.ascontiguousarray(np.asarray(L0_attn_b_out, dtype=np.float32))
    bri = np.concatenate([np.asarray(L0_ffn_bias_real), np.asarray(L0_ffn_bias_imag)]).astype(np.float32)
    g1a = np.stack([np.asarray(L0_ffn_G1_R, dtype=np.float32)[0].transpose(1, 0, 2).reshape(4, 1024),
                    np.asarray(L0_ffn_G1_I, dtype=np.float32)[0].transpose(1, 0, 2).reshape(4, 1024)])
    g2b = np.stack([np.asarray(L0_ffn_G2_R, dtype=np.float32)[:, :, 0, :].reshape(4, 256),
                    np.asarray(L0_ffn_G2_I, dtype=np.float32)[:, :, 0, :].reshape(4, 256)])
    g1a = np.ascontiguousarray(g1a); g2b = np.ascontiguousarray(g2b)
    consts = make_consts()

    nc = build(NB)
    shared = dict(wqkT=wqkT, wvT=wvT, woutT=woutT, bv=bv, bout=bout, bri=bri,
                  g1a=g1a, g2b=g2b,
                  c_idt=consts["idt"], c_tri=consts["tri"], c_perm=consts["perm"],
                  c_ssq=consts["ssq"], c_sbc=consts["sbc"], c_spad=consts["spad"],
                  c_ones64=consts["ones64"], c_cvec=consts["cvec"])
    in_maps = []
    for core in range(8):
        m = dict(shared)
        m["x"] = np.ascontiguousarray(x[core * NB:(core + 1) * NB])
        in_maps.append(m)
    res = run_bass_kernel_spmd(nc, in_maps, core_ids=list(range(8)))
    return np.concatenate([np.asarray(r["y"], dtype=np.float32) for r in res.results], axis=0)


# version 31
# speedup vs baseline: 1.0578x; 1.0317x over previous
import numpy as np
from contextlib import ExitStack
from collections import deque

import concourse.bass as bass
import concourse.mybir as mybir
from concourse.bass import AP
from concourse.bass_utils import run_bass_kernel_spmd

F32 = mybir.dt.float32
BF16 = mybir.dt.bfloat16
AF = mybir.ActivationFunctionType
OP = mybir.AluOpType
AX = mybir.AxisListType

NB = 4
T = 2048
NT = 16
NW = 4
D = 1024
H = 16
PAIRS = [(0, 1), (0, 2), (0, 3), (1, 2), (1, 3), (2, 3)]
SIG = [1.0, -1.0, 1.0, 1.0, -1.0, 1.0]
SCALE = 6.0 ** (-0.5)


def make_pi():
    pi = np.zeros(1024, dtype=np.int64)
    for p in range(1024):
        q = p % 512
        pi[p] = (p // 512) * 512 + (q % 32) * 16 + (q // 32)
    return pi


def make_consts():
    c = {}
    c["idt"] = np.eye(128, dtype=np.float32)
    c["tri"] = (np.arange(128)[None, :] >= np.arange(128)[:, None]).astype(np.float32)
    perm = np.zeros((8, 64, 96), dtype=np.float32)
    for side in range(2):
        for j in range(4):
            P = perm[side * 4 + j]
            for h in range(H):
                for cc in range(6):
                    p = 5 - cc if side == 1 else cc
                    i_, j_ = PAIRS[p]
                    comp = [i_, j_, j_, i_][j]
                    P[4 * h + comp, 6 * h + cc] = 1.0
    packed = np.zeros((4, 128, 96), dtype=np.float32)
    for s in range(2):
        packed[s * 2 + 0, 0:64] = perm[s * 4 + 0]
        packed[s * 2 + 1, 0:64] = perm[s * 4 + 2]
        packed[s * 2 + 0, 64:128] = perm[s * 4 + 1]
        packed[s * 2 + 1, 64:128] = perm[s * 4 + 3]
    c["perm"] = packed
    ssq = np.zeros((96, 16), dtype=np.float32)
    for h in range(H):
        ssq[6 * h:6 * h + 6, h] = 1.0
    c["ssq"] = ssq
    sbc = np.zeros((2, 16, 96), dtype=np.float32)
    for h in range(H):
        for cc in range(6):
            sbc[0, h, 6 * h + cc] = SIG[cc] * SCALE
            sbc[1, h, 6 * h + cc] = 1.0
    c["sbc"] = sbc
    spad = np.zeros((4, 96, 128), dtype=np.float32)
    for g in range(4):
        for m in range(4):
            for cc in range(6):
                spad[g, 6 * (4 * g + m) + cc, 32 * m + cc] = 1.0
    c["spad"] = spad
    c["ones64"] = np.ones((1, 64), dtype=np.float32)
    c["cvec"] = np.tile(np.array([1e-5, 1e-24, -0.5, -1.0], dtype=np.float32), (128, 1))
    return c


class Sem:
    def __init__(self, stack, nc, name):
        self.h = stack.enter_context(nc.semaphore(name))
        self.n = 0
    def inc(self, instr, k=1):
        instr.then_inc(self.h, k)
        self.n += k
        return self.n
    def wait(self, eng, n=None):
        eng.wait_ge(self.h, self.n if n is None else n)


def mkap(sl, extra_offset, free_dims, nparts=None):
    p = list(sl.ap[0])
    if nparts is not None:
        p[1] = nparts
    return AP(sl.tensor, sl.offset + extra_offset, [p] + [list(d) for d in free_dims])


def pi_out_ap(tens):
    return mkap(tens, 0, [[512, 2], [1, 32], [32, 16]])


def build(nb=NB, debug=False):
    nc = bass.Bass("TRN2", target_bir_lowering=False)
    def din(name, shape, dtype=F32):
        return nc.dram_tensor(name, list(shape), dtype, kind="ExternalInput").ap()

    x_d = din("x", [nb, T, D])
    wqkT_d = din("wqkT", [D, 256])
    wvT_d = din("wvT", [D, D])
    woutT_d = din("woutT", [D, D])
    bv_d = din("bv", [D])
    bout_d = din("bout", [D])
    bri_d = din("bri", [D])
    g1a_d = din("g1a", [2, 4, 1024])
    g2b_d = din("g2b", [2, 4, 256])
    idt_d = din("c_idt", [128, 128])
    tri_d = din("c_tri", [128, 128])
    perm_d = din("c_perm", [4, 128, 96])
    ssq_d = din("c_ssq", [96, 16])
    sbc_d = din("c_sbc", [2, 16, 96])
    spad_d = din("c_spad", [4, 96, 128])
    ones64_d = din("c_ones64", [1, 64])
    cvec_d = din("c_cvec", [128, 4])

    y_d = nc.dram_tensor("y", [nb, T, D], F32, kind="ExternalOutput").ap()
    tascr_d = nc.dram_tensor("tascr", [2, 1024, 256], F32, kind="Internal").ap()
    rscr_d = nc.dram_tensor("rscr", [2, 1024], F32, kind="Internal").ap()

    st = ExitStack()
    sb = lambda name, shape, dtype: st.enter_context(nc.sbuf_tensor(name, shape, dtype))
    with st:
        xnT = sb("xnT", [128, 8 * 2048], BF16)
        vaug = sb("vaug", [128, 16 * 1040], BF16)
        oT = sb("oT", [128, 8 * 2048], BF16)
        wqk_sb = sb("wqk", [128, 8 * 256], BF16)
        wv_sb = sb("wv", [128, 8 * 1024], BF16)
        wout_sb = sb("wout", [128, 8 * 1024], BF16)
        wr_sb = sb("wr", [128, 4 * 512], BF16)
        wi_sb = sb("wi", [128, 4 * 512], BF16)
        negscr = sb("negscr", [128, 512], BF16)
        bv_bc = sb("bvbc", [128, 1024], BF16)
        ball_bc = sb("ballbc", [128, 1024], F32)
        idt_sb = sb("idt", [128, 128], F32)
        tri_sb = sb("tri", [128, 128], BF16)
        perm_sb = sb("perm", [128, 4 * 96], F32)
        ssq_sb = sb("ssq", [96, 16], F32)
        sbc_sb = sb("sbc", [16, 2 * 96], F32)
        spad_sb = sb("spad", [96, 4 * 128], BF16)

        cvec = sb("cvec", [128, 4], F32)
        QK_sb = sb("qk", [128, 1024], F32)
        SCR = sb("scr", [128, 1536], F32)
        LqN = sb("lqn", [128, 2048], BF16)
        LkN = sb("lkn", [128, 2048], BF16)
        fq_pad = sb("fqp", [128, 2048], BF16)
        fk_pad = sb("fkp", [128, 2048], BF16)
        exp_sb = sb("exps", [128, 2048], BF16)
        xstage = sb("xst", [128, 1024], F32)
        xnstage = sb("xnst", [128, 1024], F32)
        x2stage = sb("x2st", [128, 1024], F32)
        ostage = sb("ost", [128, 2 * 1024], F32)
        cdiv = sb("cdiv", [64, 1024], F32)
        bscr = sb("bscr", [16, 512], F32)
        stat = sb("stat", [128, 8], F32)

        S_ps = st.enter_context(nc.psum_tensor("S", [128, 2048], F32))
        O4 = st.enter_context(nc.psum_tensor("O4", [65, 1024], F32))
        F6 = st.enter_context(nc.psum_tensor("F6", [128, 512], F32))
        F7 = st.enter_context(nc.psum_tensor("F7", [128, 512], F32))
        FB = [F6, F7]

        sems = {}
        for name in ["dwp", "dxp", "dws", "dyt", "dx", "dx2", "dyp", "dya", "dbo", "dbi", "dtr", "dyb", "t", "v", "s"]:
            sems[name] = Sem(st, nc, "sem_" + name)
        dwp, dxp, dws, dyt = sems["dwp"], sems["dxp"], sems["dws"], sems["dyt"]
        dx, dx2, dyp, dya = sems["dx"], sems["dx2"], sems["dyp"], sems["dya"]
        dbo, dbi = sems["dbo"], sems["dbi"]
        dtr, dyb = sems["dtr"], sems["dyb"]
        DYA = [dya, dyb]
        tS, vS, sS = sems["t"], sems["v"], sems["s"]
        TE, VE, SE, GE, SY = nc.tensor, nc.vector, nc.scalar, nc.gpsimd, nc.sync

        stx = {
            "xst_rs": [0, 0],
            "x2_r": 0,
            "xn_r": 0,
            "f_w": [0, 0],
            "f_r": [0, 0],
            "f_rs": [0, 0],
            "qk_w": 0,
            "xnT_tile": [0] * NT,
            "lq_done": 0,
            "vaug_grp": [0] * 4,
            "sfree_s": [0, 0],
            "sfree_v": [0, 0],
            "o4_free": 0,
            "ot_r": 0,
            "odiv": 0,
            "exp_last": 0,
            "av_prev": [0, 0],
            "ost_rp": [0, 0],
            "ost_ra": [0, 0],
            "dyp_last": 0, "dya_last_p": [0, 0],
            "unfold_t": 0,
        }

        gloads = [(tri_sb[:], tri_d),
                  (bv_bc[:], bv_d.rearrange("(one d) -> one d", one=1).broadcast_to([128, 1024]))]
        for kc in range(8):
            gloads.append((wqk_sb[:, kc * 256:(kc + 1) * 256], wqkT_d[kc * 128:(kc + 1) * 128, :]))
            gloads.append((wv_sb[:, kc * 1024:(kc + 1) * 1024], wvT_d[kc * 128:(kc + 1) * 128, :]))
            gloads.append((wout_sb[:, kc * 1024:(kc + 1) * 1024], woutT_d[kc * 128:(kc + 1) * 128, :]))
        for g in range(4):
            gloads.append((spad_sb[:, g * 128:(g + 1) * 128], spad_d[g]))
        for out_ap, in_ap in gloads:
            dwp.inc(GE.dma_start(out_ap, in_ap), 16)
        sloads = [
            (idt_sb[:], idt_d),
            (ssq_sb[:], ssq_d),
            (cvec[:], cvec_d),
            (ostage[:, 0:1024], bout_d.rearrange("(one d) -> one d", one=1).broadcast_to([128, 1024])),
            (ostage[:, 1024:2048], bri_d.rearrange("(one d) -> one d", one=1).broadcast_to([128, 1024])),
            (SCR[0:4, 0:1024], g1a_d[0]),
            (cdiv[0:4, 0:1024], g1a_d[1]),
            (QK_sb[0:4, 512:768], g2b_d[0]),
            (QK_sb[0:4, 768:1024], g2b_d[1]),
        ]
        for s8 in range(4):
            sloads.append((perm_sb[:, s8 * 96:(s8 + 1) * 96], perm_d[s8]))
        for s2 in range(2):
            sloads.append((sbc_sb[:, s2 * 96:(s2 + 1) * 96], sbc_d[s2]))
        for out_ap, in_ap in sloads:
            dxp.inc(SY.dma_start(out_ap, in_ap), 16)
        dxp.wait(VE)
        vS.inc(VE.tensor_tensor(ball_bc[:], ostage[:, 0:1024], ostage[:, 1024:2048], OP.add))
        dxp.wait(TE)
        dxp.wait(SE)

        def unf_step(ri, mt):
            half = mt % 2
            Fb = FB[half]
            qp = QK_sb[:, half * 256:(half + 1) * 256]
            dsem = dyt if half == 0 else dtr
            vS.wait(TE, stx["f_r"][half])
            sS.wait(TE, stx["f_rs"][half])
            src_g = SCR if ri == 0 else cdiv
            tS.inc(TE.matmul(Fb[:, 0:256],
                             src_g[0:4, mt * 128:(mt + 1) * 128],
                             QK_sb[0:4, 512 + ri * 256: 768 + ri * 256],
                             start=True, stop=True))
            stx["f_w"][half] = tS.n
            stx["unfold_t"] = tS.n
            tS.wait(VE, stx["f_w"][half])
            dsem.wait(VE)
            stx["f_r"][half] = vS.inc(VE.tensor_copy(qp, mkap(Fb[:, 0:256], 0, [[1, 16], [16, 16]])))
            vS.wait(GE, stx["f_r"][half])
            dsem.inc(GE.dma_start(tascr_d[ri, mt * 128:(mt + 1) * 128, :], qp), 16)

        def unf_fin():
            dyt.wait(GE)
            dtr.wait(GE)
            for ri, wdst in [(0, wr_sb), (1, wi_sb)]:
                base = tascr_d[ri]
                for g in range(16):
                    in_ap = AP(base.tensor, base.offset + g * 16, [[256, 32], [8192, 32], [1, 16]])
                    out_sl = wdst[(g % 4) * 32:(g % 4) * 32 + 32,
                                  (g // 4) * 512:(g // 4) * 512 + 512]
                    dws.inc(GE.dma_start(mkap(out_sl, 0, [[16, 32], [1, 16]]), in_ap), 16)

        unf_steps = deque([(lambda ri=ri, mt=mt: unf_step(ri, mt))
                           for ri in range(2) for mt in range(8)] + [unf_fin])

        def A1_tile(b1, tt, slot=0):
            xs = xstage[:] if slot == 0 else exp_sb[:, 0:2048].bitcast(F32)
            dxs = dx if slot == 0 else dx2
            vS.wait(SY, stx["xst_rs"][slot])
            dxs.inc(SY.dma_start(xs, x_d[b1, tt * 128:(tt + 1) * 128, :]), 16)
            dxs.wait(VE)
            if slot == 0:
                dx2.wait(VE)
            vS.inc(VE.tensor_tensor(x2stage[:], xs, xs, OP.mult))
            vS.inc(VE.reduce_sum(stat[:, 0:1], xs, axis=AX.X))
            vS.wait(VE)
            vS.inc(VE.reduce_sum(stat[:, 1:2], x2stage[:], axis=AX.X))
            stx["x2_r"] = vS.n
            vS.inc(VE.tensor_scalar(stat[:, 2:3], stat[:, 0:1], 1.0 / 1024, None, OP.mult))
            vS.wait(VE)
            vS.inc(VE.tensor_scalar(stat[:, 3:4], stat[:, 1:2], 1.0 / 1024, None, OP.mult))
            vS.inc(VE.tensor_tensor(stat[:, 4:5], stat[:, 2:3], stat[:, 2:3], OP.mult))
            vS.wait(VE)
            var_s = vS.inc(VE.tensor_tensor(stat[:, 5:6], stat[:, 3:4], stat[:, 4:5], OP.subtract))
            vS.wait(SE, var_s)
            sS.inc(SE.activation(stat[:, 6:7], stat[:, 5:6], AF.Ln, bias=cvec[:, 0:1]))
            sS.wait(SE)
            rstd_s = sS.inc(SE.activation(stat[:, 7:8], stat[:, 6:7], AF.Exp, scale=cvec[:, 2:3]))
            sS.wait(VE, rstd_s)
            vS.wait(VE)
            tS.wait(VE, stx.get("xn_r", 0))
            dx2.wait(VE)
            xn_s = vS.inc(VE.tensor_scalar(pi_out_ap(xnstage[:]), xs, stat[:, 2:3],
                                           stat[:, 7:8], OP.subtract, OP.mult))
            stx["xst_rs"][slot] = xn_s
            stx["a_xn"] = xn_s

        def A2_tile(b1, tt):
            xn_s = stx["a_xn"]
            for half in range(2):
                Fb = FB[half]
                vS.wait(TE, max(xn_s, stx["f_r"][half]))
                sS.wait(TE, stx["f_rs"][half])
                for c4 in range(4):
                    c = half * 4 + c4
                    tS.inc(TE.transpose(Fb[:, c4 * 128:(c4 + 1) * 128],
                                        xnstage[:, c * 128:(c + 1) * 128], idt_sb[:]))
                stx["f_w"][half] = tS.n
                tS.wait(VE, stx["f_w"][half])
                out_ap = mkap(xnT[:], (half * 4) * 2048 + tt * 128, [[2048, 4], [1, 128]])
                stx["f_r"][half] = vS.inc(VE.tensor_copy(out_ap, Fb[:, 0:512]))
            stx["xn_r"] = tS.n
            stx["xnT_tile"][tt] = stx["f_r"][1]

        def B1_chunk(b1, ch):
            if not stx.get("gl_ok"):
                dwp.wait(TE)
                dwp.wait(VE)
                dyt.wait(VE)
                dtr.wait(VE)
                stx["gl_ok"] = True
            vS.wait(TE, max(stx["xnT_tile"][4 * ch + 3], stx["f_r"][0], stx["f_r"][1]))
            sS.wait(TE, max(stx["f_rs"][0], stx["f_rs"][1]))
            for mt in range(2):
                for kc in range(8):
                    ins = TE.matmul(FB[mt][:, 0:512],
                                    wqk_sb[:, kc * 256 + mt * 128: kc * 256 + mt * 128 + 128],
                                    xnT[:, kc * 2048 + ch * 512: kc * 2048 + ch * 512 + 512],
                                    start=(kc == 0), stop=(kc == 7))
                tS.inc(ins)
                stx["f_w"][mt] = tS.n
            for mt in range(2):
                tS.wait(VE, stx["f_w"][mt])
                stx["f_r"][mt] = vS.inc(VE.tensor_copy(QK_sb[:, mt * 512:(mt + 1) * 512],
                                                       FB[mt][:, 0:512]))
            stx["qk_w"] = vS.n

        def B2_chunk(b1, ch, s):
            c0, c1, c2 = (SCR[0:96, i * 512:(i + 1) * 512] for i in range(3))
            c3 = c2
            if True:
                a1 = QK_sb[0:64, s * 512:s * 512 + 512]
                a2 = QK_sb[64:128, s * 512:s * 512 + 512]
                vS.wait(TE, max(stx["qk_w"], stx["f_r"][0], stx["f_r"][1]))
                sS.wait(TE, max(stx["f_rs"][0], stx["f_rs"][1]))
                pU = perm_sb[0:64, (s * 2 + 0) * 96:(s * 2 + 0) * 96 + 96]
                pV = perm_sb[64:128, (s * 2 + 0) * 96:(s * 2 + 0) * 96 + 96]
                w6 = tS.inc(TE.matmul(F6[0:96, 0:512], pU, a1, start=True, stop=True))
                w7 = tS.inc(TE.matmul(F7[0:96, 0:512], pV, a2, start=True, stop=True))
                tS.wait(VE, w6)
                vS.inc(VE.tensor_copy(c0, F6[0:96, 0:512]))
                stx["f_r"][0] = vS.n
                tS.wait(VE, w7)
                vS.wait(VE)
                vS.inc(VE.tensor_tensor(c1, c0, F7[0:96, 0:512], OP.mult))
                stx["f_r"][1] = vS.n
                vS.wait(TE, max(stx["f_r"][0], stx["f_r"][1]))
                pU2 = perm_sb[0:64, (s * 2 + 1) * 96:(s * 2 + 1) * 96 + 96]
                pV2 = perm_sb[64:128, (s * 2 + 1) * 96:(s * 2 + 1) * 96 + 96]
                w6 = tS.inc(TE.matmul(F6[0:96, 0:512], pU2, a1, start=True, stop=True))
                w7 = tS.inc(TE.matmul(F7[0:96, 0:512], pV2, a2, start=True, stop=True))
                tS.wait(VE, w6)
                vS.inc(VE.tensor_copy(c2, F6[0:96, 0:512]))
                stx["f_r"][0] = vS.n
                tS.wait(VE, w7)
                vS.wait(VE)
                vS.inc(VE.tensor_tensor(c0, c2, F7[0:96, 0:512], OP.mult))
                stx["f_r"][1] = vS.n
                vS.wait(VE)
                vS.inc(VE.tensor_tensor(c3, c1, c0, OP.subtract))
                vS.wait(VE)
                l2_s = vS.inc(VE.tensor_tensor(c0, c3, c3, OP.mult))
                vS.wait(TE, max(l2_s, stx["f_r"][0]))
                n2_s = tS.inc(TE.matmul(F6[0:16, 0:512], ssq_sb[:], c0, start=True, stop=True))
                tS.wait(SE, n2_s)
                sS.inc(SE.activation(SCR[96:112, 0:512], F6[0:16, 0:512], AF.Ln, bias=cvec[96:112, 1:2]))
                stx["f_rs"][0] = sS.n
                sS.wait(SE)
                rn_s = sS.inc(SE.activation(bscr[:], SCR[96:112, 0:512], AF.Exp,
                                            scale=cvec[96:112, 2:3]))
                sS.wait(TE, rn_s)
                vS.wait(TE, stx["f_r"][1])
                bcn_s = tS.inc(TE.matmul(F7[0:96, 0:512], sbc_sb[:, s * 96:s * 96 + 96],
                                         bscr[:], start=True, stop=True))
                tS.wait(VE, bcn_s)
                vS.wait(VE)
                dst = (LqN if s == 0 else LkN)
                stx["f_r"][1] = vS.inc(VE.tensor_tensor(dst[0:96, ch * 512:ch * 512 + 512],
                                                        c3, F7[0:96, 0:512], OP.mult))
            if ch == 3 and s == 1:
                stx["lq_done"] = vS.n

        def V_group(b1, g, cav_snap):
            for tt in range(NT):
                p = tt % 2
                Fb = FB[p]
                vS.wait(TE, max(stx["f_r"][p], stx["xnT_tile"][tt]))
                sS.wait(TE, stx["f_rs"][p])
                for kc in range(8):
                    ins = TE.matmul(Fb[:, 0:256],
                                    xnT[:, kc * 2048 + tt * 128: kc * 2048 + tt * 128 + 128],
                                    wv_sb[:, kc * 1024 + g * 256: kc * 1024 + g * 256 + 256],
                                    start=(kc == 0), stop=(kc == 7))
                tS.inc(ins)
                stx["f_w"][p] = tS.n
                tS.wait(VE, max(stx["f_w"][p], cav_snap))
                base = tt * 1040 + (4 * g) * 65
                vaug_ap = mkap(vaug[:], base, [[65, 4], [1, 64]])
                bvs = mkap(bv_bc[:], g * 256, [[64, 4], [1, 64]])
                vS.inc(VE.tensor_tensor(vaug_ap, Fb[:, 0:256], bvs, OP.add))
                ones_ap = mkap(vaug[:], base + 64, [[65, 4]])
                stx["f_r"][p] = vS.inc(VE.memset(ones_ap, 1.0))
            stx["vaug_grp"][g] = vS.n

        def FFN_pass1(b1, tt):
            if not stx.get("w_ok"):
                dws.wait(TE)
                stx["w_ok"] = True
            vS.wait(SY, stx["x2_r"])
            dx2.inc(SY.dma_start(x2stage[:], x_d[b1, tt * 128:(tt + 1) * 128, :]), 16)
            dx2s = dx2.n
            vS.wait(TE, max(stx["f_r"][0], stx["f_r"][1], stx["xnT_tile"][tt]))
            sS.wait(TE, max(stx["f_rs"][0], stx["f_rs"][1]))
            vS.wait(VE, stx["xnT_tile"][tt])
            tS.wait(VE, stx.get("neg_r", 0))
            for kc in range(4):
                vS.inc(VE.tensor_scalar(negscr[:, kc * 128:(kc + 1) * 128],
                                        xnT[:, (kc + 4) * 2048 + tt * 128: (kc + 4) * 2048 + tt * 128 + 128],
                                        -1.0, None, OP.mult))
            neg_s = vS.n
            vS.wait(TE, neg_s)
            for kc in range(8):
                rhs = (wr_sb if kc < 4 else wi_sb)[:, (kc % 4) * 512:(kc % 4) * 512 + 512]
                lhsT = (xnT[:, kc * 2048 + tt * 128: kc * 2048 + tt * 128 + 128] if kc < 4
                        else negscr[:, (kc - 4) * 128:(kc - 3) * 128])
                ins = TE.matmul(F6[:, 0:512], lhsT,
                                rhs, start=(kc == 0), stop=(kc == 7))
            tS.inc(ins)
            stx["f_w"][0] = tS.n
            stx["neg_r"] = tS.n
            for kc in range(8):
                rhs = (wi_sb if kc < 4 else wr_sb)[:, (kc % 4) * 512:(kc % 4) * 512 + 512]
                ins = TE.matmul(F7[:, 0:512],
                                xnT[:, kc * 2048 + tt * 128: kc * 2048 + tt * 128 + 128],
                                rhs, start=(kc == 0), stop=(kc == 7))
            tS.inc(ins)
            stx["f_w"][1] = tS.n
            tS.wait(VE, stx["f_w"][0])
            dyp.wait(VE, stx["ost_rp"][0])
            dya.wait(VE, stx["ost_ra"][0])
            vS.inc(VE.tensor_tensor(ostage[:, 0:512], F6[:, 0:512], ball_bc[:, 0:512], OP.add))
            stx["f_r"][0] = vS.n
            tS.wait(VE, stx["f_w"][1])
            vS.inc(VE.tensor_tensor(ostage[:, 512:1024], F7[:, 0:512], ball_bc[:, 512:1024], OP.add))
            stx["f_r"][1] = vS.n
            dx2.wait(VE, dx2s)
            dyp.wait(VE, stx["ost_rp"][1])
            dyb.wait(VE, stx["ost_ra"][1])
            vS.wait(VE)
            o2_s = vS.inc(VE.tensor_tensor(ostage[:, 1024:2048], ostage[:, 0:1024], x2stage[:], OP.add))
            stx["x2_r"] = o2_s
            vS.wait(GE, o2_s)
            dyp.wait(GE, stx["dyp_last"])
            stx["dyp_last"] = dyp.inc(GE.dma_start(y_d[b1, tt * 128:(tt + 1) * 128, :],
                                                   ostage[:, 1024:2048]), 16)
            stx["ost_rp"][1] = stx["dyp_last"]

        def burst_outproj(b):
            for tt in range(NT):
                p = tt % 2
                Pp = S_ps[:, p * 1024:(p + 1) * 1024]
                sS.wait(TE, stx["exp_last"])
                vS.wait(TE, max(stx["odiv"], stx["sfree_v"][p]))
                for half in range(2):
                    for pt in range(8):
                        ins = TE.matmul(Pp[:, half * 512:half * 512 + 512],
                                        oT[:, pt * 2048 + tt * 128: pt * 2048 + tt * 128 + 128],
                                        wout_sb[:, pt * 1024 + half * 512: pt * 1024 + half * 512 + 512],
                                        start=(pt == 0), stop=(pt == 7))
                tS.inc(ins)
                pw = tS.n
                tS.wait(VE, pw)
                dyp.wait(VE, stx["ost_rp"][p])
                DYA[p].wait(VE, stx["ost_ra"][p])
                cp_s = vS.inc(VE.tensor_copy(ostage[:, p * 1024:(p + 1) * 1024], Pp))
                stx["sfree_v"][p] = cp_s
                vS.wait(GE, cp_s)
                DYA[p].wait(GE, stx["dya_last_p"][p])
                stx["dya_last_p"][p] = DYA[p].inc(GE.dma_start(y_d[b, tt * 128:(tt + 1) * 128, :],
                                                               ostage[:, p * 1024:(p + 1) * 1024],
                                                               accum_op=OP.add), 16)
                stx["ost_ra"][p] = stx["dya_last_p"][p]
            stx["ot_r"] = tS.n

        def C_scatter(g):
            for s, (srcF, dstp) in enumerate([(LqN, fq_pad), (LkN, fk_pad)]):
                for ch in range(4):
                    pset = (ch + 4 * s) % 2
                    sS.wait(TE, stx["sfree_s"][pset])
                    vS.wait(TE, max(stx["lq_done"], stx["sfree_v"][pset]))
                    w_s = tS.inc(TE.matmul(S_ps[:, pset * 1024:pset * 1024 + 512],
                                           spad_sb[:, g * 128:(g + 1) * 128],
                                           srcF[0:96, ch * 512:ch * 512 + 512],
                                           start=True, stop=True))
                    tS.wait(VE, w_s)
                    stx["sfree_v"][pset] = vS.inc(
                        VE.tensor_copy(dstp[:, ch * 512:ch * 512 + 512],
                                       S_ps[:, pset * 1024:pset * 1024 + 512]))

        def C_pair(b, g, pr, w, pace=None):
            if not stx.get("xbuf_ok"):
                vS.wait(SE, stx["xst_rs"][1])
                stx["xbuf_ok"] = True
            last_kt = 4 * w + 3
            nkt = 4 * w + 4
            exp_snap = {}
            av_snap = {-2: stx["av_prev"][0], -1: stx["av_prev"][1]}
            sc_snap = {}

            def emit_scores(kt):
                ss = kt % 2
                soff = max(kt * 128 - w * 512, 0)
                if kt >= 2:
                    sS.wait(TE, exp_snap[kt - 2])
                else:
                    sS.wait(TE, stx["sfree_s"][ss])
                    vS.wait(TE, max(stx["sfree_v"][ss], stx["vaug_grp"][g]))
                for j in range(2):
                    m4 = 2 * pr + j
                    tS.inc(TE.matmul(
                        S_ps[:, ss * 1024 + j * 512 + soff: ss * 1024 + j * 512 + 512],
                        fk_pad[32 * m4:32 * m4 + 6, kt * 128:kt * 128 + 128],
                        fq_pad[32 * m4:32 * m4 + 6, w * 512 + soff:w * 512 + 512],
                        start=True, stop=True, tile_position=(32 * m4, 0)))
                sc_snap[kt] = tS.n

            emit_scores(0)
            if nkt > 1:
                emit_scores(1)
            if stx.get("pending_div") is not None:
                stx["pending_div"]()
                stx["pending_div"] = None
            for kt in range(nkt):
                off = max(kt * 128 - w * 512, 0)
                ss = kt % 2
                tS.wait(SE, max(av_snap[kt - 2], sc_snap[kt]))
                if off:
                    e_out = mkap(exp_sb[:], ss * 1024 + off, [[512, 2], [1, 512 - off]])
                    e_in = mkap(S_ps[:], ss * 1024 + off, [[512, 2], [1, 512 - off]])
                else:
                    e_out = exp_sb[:, ss * 1024: ss * 1024 + 1024]
                    e_in = S_ps[:, ss * 1024: ss * 1024 + 1024]
                sS.inc(SE.activation(e_out, e_in, AF.Exp))
                exp_snap[kt] = sS.n
                if kt + 2 < nkt:
                    emit_scores(kt + 2)
                if kt >= 4 * w:
                    sS.wait(VE, exp_snap[kt])
                    in0 = mkap(exp_sb[:], ss * 1024 + off, [[512, 2], [1, 128]])
                    tri_b = mkap(tri_sb[:], 0, [[0, 2], [1, 128]])
                    mk = vS.inc(VE.tensor_tensor(in0, in0, tri_b, OP.mult))
                    vS.wait(TE, mk)
                elif kt + 2 >= nkt:
                    sS.wait(TE, exp_snap[kt])
                if kt == 0:
                    vS.wait(TE, stx["o4_free"])
                for j in range(2):
                    m4 = 2 * pr + j
                    h = 4 * g + m4
                    ins = TE.matmul(
                        O4[:, j * 512 + off:(j + 1) * 512],
                        vaug[:, kt * 1040 + h * 65: kt * 1040 + h * 65 + 65],
                        exp_sb[:, ss * 1024 + j * 512 + off: ss * 1024 + (j + 1) * 512],
                        start=(kt == 0), stop=(kt == last_kt))
                tS.inc(ins)
                av_snap[kt] = tS.n
                if pace is not None:
                    pace()
            stx["av_prev"] = [av_snap[nkt - 2], av_snap[nkt - 1]]
            stx["sfree_s"][0] = exp_snap[nkt - 2]
            stx["sfree_s"][1] = exp_snap[nkt - 1]
            stx["exp_last"] = sS.n
            av_last = av_snap[last_kt]

            def emit_divide(g=g, pr=pr, w=w, av_last=av_last):
                if not stx.get("cdiv_ok"):
                    tS.wait(SE, stx["unfold_t"])
                    tS.wait(SY, stx["unfold_t"])
                    stx["cdiv_ok"] = True
                slot = stx.get("rb_slot", 0)
                stx["rb_slot"] = 1 - slot
                tS.wait(SE, av_last)
                vS.wait(SE, stx["o4_free"])
                sS.inc(SE.activation(cdiv[32:33, 0:1024], O4[64:65, 0:1024], AF.Ln))
                sS.wait(SE)
                r_s = sS.inc(SE.activation(cdiv[0:1, 0:1024], cdiv[32:33, 0:1024], AF.Exp,
                                           scale=cvec[0:1, 3:4]))
                sS.wait(SY, r_s)
                dbi.wait(SY)
                dbo.wait(SY)
                dbo.inc(SY.dma_start(rscr_d[slot].rearrange("(one n) -> one n", one=1),
                                     cdiv[0:1, 0:1024]), 16)
                vS.wait(SY, stx["o4_free"])
                dbo.wait(SY)
                bcast = rscr_d[slot].rearrange("(one n) -> one n", one=1).broadcast_to([64, 1024])
                bin_s = dbi.inc(SY.dma_start(cdiv[0:64, 0:1024], bcast), 16)
                dbi.wait(VE, bin_s)
                tS.wait(VE, stx["ot_r"])
                for j in range(2):
                    h = 4 * g + 2 * pr + j
                    pt, rr = h // 2, (h % 2) * 64
                    vS.inc(VE.tensor_tensor(
                        oT[rr:rr + 64, pt * 2048 + w * 512: pt * 2048 + w * 512 + 512],
                        O4[0:64, j * 512:j * 512 + 512],
                        cdiv[0:64, j * 512:j * 512 + 512], OP.mult))
                stx["o4_free"] = vS.n
                stx["odiv"] = vS.n

            stx["pending_div"] = emit_divide

        for tt in range(NT):
            A1_tile(0, tt, tt % 2)
            A2_tile(0, tt)
            for _ in range(2):
                if unf_steps:
                    unf_steps.popleft()()
        for ch in range(4):
            B1_chunk(0, ch)
            B2_chunk(0, ch, 0)
            B2_chunk(0, ch, 1)
        for g in range(4):
            V_group(0, g, 0)
        for tt in range(NT):
            FFN_pass1(0, tt)

        q1 = deque()
        q2 = deque()
        for b in range(nb):
            if b + 1 < nb:
                for tt in range(NT):
                    q1.append((0, lambda tt=tt, b1=b + 1: A1_tile(b1, tt)))
                    q1.append((1, lambda tt=tt, b1=b + 1: A2_tile(b1, tt)))
                for tt in range(NT):
                    q1.append((1, lambda tt=tt, b1=b + 1: FFN_pass1(b1, tt)))
                for ch in range(4):
                    q2.append((1, lambda ch=ch, b1=b + 1: B1_chunk(b1, ch)))
                    q2.append((1, lambda ch=ch, b1=b + 1: B2_chunk(b1, ch, 0)))
                    q2.append((1, lambda ch=ch, b1=b + 1: B2_chunk(b1, ch, 1)))
            state = {"step": 0, "gate": False, "popped": 0.0}
            total_items = sum(w for w, _ in q1) + sum(w for w, _ in q2) + (4 if b + 1 < nb else 0)
            def pace():
                state["step"] += 1
                target = state["step"] * total_items / 320.0
                while state["popped"] < target:
                    if q1:
                        w_, fn = q1.popleft()
                    elif state["gate"] and q2:
                        w_, fn = q2.popleft()
                    else:
                        break
                    fn()
                    state["popped"] += max(w_, 0.5)
            for g in range(4):
                C_scatter(g)
                if g == 3:
                    state["gate"] = True
                for pr in range(2):
                    for w in range(NW):
                        C_pair(b, g, pr, w, pace)
                if b + 1 < nb:
                    q2.append((1, lambda g=g, b1=b + 1, snap=tS.n: V_group(b1, g, snap)))
            if stx.get("pending_div") is not None:
                stx["pending_div"]()
                stx["pending_div"] = None
            while q1:
                q1.popleft()[1]()
            while q2:
                q2.popleft()[1]()
            burst_outproj(b)

        dyp.wait(SY)
        dya.wait(SY)
        dyb.wait(SY)
    return nc



def kernel(L0_attn_x, L0_attn_W1_q, L0_attn_W2_q, L0_attn_W1_k, L0_attn_W2_k,
           L0_attn_W_v, L0_attn_b_v, L0_attn_W_out, L0_attn_b_out,
           L0_ffn_G1_R, L0_ffn_G2_R, L0_ffn_G1_I, L0_ffn_G2_I,
           L0_ffn_bias_real, L0_ffn_bias_imag):
    x = np.ascontiguousarray(np.asarray(L0_attn_x, dtype=np.float32))
    pi = make_pi()
    Wqk = np.concatenate([np.asarray(L0_attn_W1_q), np.asarray(L0_attn_W2_q),
                          np.asarray(L0_attn_W1_k), np.asarray(L0_attn_W2_k)], axis=0).astype(np.float32)
    wqkT = np.ascontiguousarray(Wqk[:, pi].T)
    wvT = np.ascontiguousarray(np.asarray(L0_attn_W_v, dtype=np.float32)[:, pi].T)
    woutT = np.ascontiguousarray(np.asarray(L0_attn_W_out, dtype=np.float32).T)
    bv = np.ascontiguousarray(np.asarray(L0_attn_b_v, dtype=np.float32))
    bout = np.ascontiguousarray(np.asarray(L0_attn_b_out, dtype=np.float32))
    bri = np.concatenate([np.asarray(L0_ffn_bias_real), np.asarray(L0_ffn_bias_imag)]).astype(np.float32)
    g1a = np.stack([np.asarray(L0_ffn_G1_R, dtype=np.float32)[0].transpose(1, 0, 2).reshape(4, 1024),
                    np.asarray(L0_ffn_G1_I, dtype=np.float32)[0].transpose(1, 0, 2).reshape(4, 1024)])
    g2b = np.stack([np.asarray(L0_ffn_G2_R, dtype=np.float32)[:, :, 0, :].reshape(4, 256),
                    np.asarray(L0_ffn_G2_I, dtype=np.float32)[:, :, 0, :].reshape(4, 256)])
    g1a = np.ascontiguousarray(g1a); g2b = np.ascontiguousarray(g2b)
    consts = make_consts()

    nc = build(NB)
    shared = dict(wqkT=wqkT, wvT=wvT, woutT=woutT, bv=bv, bout=bout, bri=bri,
                  g1a=g1a, g2b=g2b,
                  c_idt=consts["idt"], c_tri=consts["tri"], c_perm=consts["perm"],
                  c_ssq=consts["ssq"], c_sbc=consts["sbc"], c_spad=consts["spad"],
                  c_ones64=consts["ones64"], c_cvec=consts["cvec"])
    in_maps = []
    for core in range(8):
        m = dict(shared)
        m["x"] = np.ascontiguousarray(x[core * NB:(core + 1) * NB])
        in_maps.append(m)
    res = run_bass_kernel_spmd(nc, in_maps, core_ids=list(range(8)))
    return np.concatenate([np.asarray(r["y"], dtype=np.float32) for r in res.results], axis=0)
